# version 18
# baseline (speedup 1.0000x reference)
import numpy as np
import concourse.bass as bass
import concourse.mybir as mybir
from concourse.bass_utils import run_bass_kernel_spmd

F32 = mybir.dt.float32
BF16 = mybir.dt.bfloat16
AF = mybir.ActivationFunctionType
ALU = mybir.AluOpType
AX = mybir.AxisListType

D = 1024
T = 2048
NSEQ = 2
NC = 8
DFF = 4096
NE = 8
EH = 256
TT = 512
NTT = T // TT
HL = 1024
LN_EPS = 1e-5
RMS_EPS = 1e-6
SWG = 4

NSL = T // HL


class Slot:
    __slots__ = ("w", "r")

    def __init__(self):
        self.w = None
        self.r = []


DMA_POOL = {"d": [f"sd{i}" for i in range(8)],
            "g": [f"sg{i}" for i in range(4)],
            "o": [f"so{i}" for i in range(2)]}
ALL_SEMS = (["st", "sv", "ss", "sgc"]
            + DMA_POOL["d"] + DMA_POOL["g"] + DMA_POOL["o"])


class Sched:
    def __init__(self):
        self.prog = {e: [] for e in ("t", "v", "s", "d", "g")}
        self.semval = {s: 0 for s in ALL_SEMS}
        self.wm = {e: {} for e in self.prog}
        self.rr = {k: 0 for k in DMA_POOL}

    def emit(self, eng, build, reads=(), writes=(), inc=1, sem=None):
        need = {}

        def add(tok):
            if tok is not None:
                s, v = tok
                if need.get(s, 0) < v:
                    need[s] = v

        for sl in reads:
            add(sl.w)
        for sl in writes:
            add(sl.w)
            for tok in sl.r:
                add(tok)
        is_dma = inc == 16
        if is_dma:
            key = "o" if sem == "o" else eng
            pool = DMA_POOL[key]
            s_ = pool[self.rr[key] % len(pool)]
            self.rr[key] += 1
            if self.semval[s_] > 0:
                add((s_, self.semval[s_]))
            sem = s_
        elif sem is None:
            sem = {"t": "st", "v": "sv", "s": "ss", "g": "sgc"}[eng]
        wm = self.wm[eng]
        waits = sorted((s, v) for s, v in need.items() if v > wm.get(s, 0))
        for s, v in need.items():
            if v > wm.get(s, 0):
                wm[s] = v
        self.semval[sem] += inc
        tok = (sem, self.semval[sem])
        self.prog[eng].append((waits, build, sem, inc))
        for sl in writes:
            sl.w = tok
            sl.r = []
        for sl in reads:
            sl.r.append(tok)
        return tok

    def final_output_waits(self, eng):
        for s_ in DMA_POOL["o"]:
            if self.semval[s_] > 0:
                self.prog[eng].append(([(s_, self.semval[s_])], None, None, 0))


def hand_off(dst_slots, src_slots):
    toks = []
    for s in src_slots:
        if s.w is not None:
            toks.append(s.w)
        toks += s.r
        s.w = None
        s.r = []
    for d in dst_slots:
        d.r = d.r + toks


def build_program(upto=9):
    nc = bass.Bass("TRN2", target_bir_lowering=False)
    sch = Sched()

    xin = nc.dram_tensor("xin", [NSEQ, D, T], F32, kind="ExternalInput")
    a_wg = nc.dram_tensor("a_wg", [NC, 128, D], F32, kind="ExternalInput")
    a_wc = nc.dram_tensor("a_wc", [NC, 128, D], F32, kind="ExternalInput")
    a_w1 = nc.dram_tensor("a_w1", [DFF // 128, 128, D], F32, kind="ExternalInput")
    a_w3 = nc.dram_tensor("a_w3", [DFF // 128, 128, D], F32, kind="ExternalInput")
    a_w2 = nc.dram_tensor("a_w2", [DFF, D], F32, kind="ExternalInput")
    f_w1 = nc.dram_tensor("f_w1", [NE, 2, 128, D], F32, kind="ExternalInput")
    f_w2 = nc.dram_tensor("f_w2", [NE, EH, D], F32, kind="ExternalInput")
    f_b2 = nc.dram_tensor("f_b2", [NE, D], F32, kind="ExternalInput")
    pk_t = nc.dram_tensor("pk", [128, 176], F32, kind="ExternalInput")
    wgt_t = nc.dram_tensor("wgt", [128, 64], F32, kind="ExternalInput")
    ident = nc.dram_tensor("ident", [128, 128], F32, kind="ExternalInput")
    selb = nc.dram_tensor("selb", [8, 1024], F32, kind="ExternalInput")
    yout = nc.dram_tensor("yout", [NSEQ, D, T], F32, kind="ExternalOutput")

    xr = nc.alloc_sbuf_tensor("xr", [128, NC * T], F32)
    xmm = nc.alloc_sbuf_tensor("xmm", [128, NC * T], BF16)
    wa = nc.alloc_sbuf_tensor("wa", [128, 16384], BF16)
    ghb = nc.alloc_sbuf_tensor("ghb", [128, 16384], BF16)
    zb = nc.alloc_sbuf_tensor_at("zb", [128, NC * HL], F32,
                                 offset=nc.lookup_mloc(ghb).addr)
    sc2 = nc.alloc_sbuf_tensor("sc2", [128, 2 * HL], F32)
    lgsb = nc.alloc_sbuf_tensor_at("lgsb", [8, TT], F32,
                                   offset=nc.lookup_mloc(sc2).addr)
    stg = nc.alloc_sbuf_tensor("stg", [128, 2048], F32)
    htmp = nc.alloc_sbuf_tensor("htmp", [128, 2048], BF16)
    rows = nc.alloc_sbuf_tensor("rows", [1, 4 * TT], F32)
    m2row = nc.alloc_sbuf_tensor("m2row", [1, T], BF16)
    cs1 = nc.alloc_sbuf_tensor("cs1", [1, NE * EH], BF16)
    mfm = nc.alloc_sbuf_tensor("mfm", [8, T], BF16)
    csgb = nc.alloc_sbuf_tensor("csgb", [1, 8], BF16)
    wgtf = nc.alloc_sbuf_tensor("wgtf", [128, 64], F32)
    wgtb = nc.alloc_sbuf_tensor("wgtb", [128, 64], BF16)
    b2b = nc.alloc_sbuf_tensor("b2b", [8, D], BF16)
    bias1 = nc.alloc_sbuf_tensor("bias1", [128, 16], F32)
    cary = nc.alloc_sbuf_tensor("cary", [128, 8], F32)
    cols = nc.alloc_sbuf_tensor("cols", [128, 180], F32)
    idn = nc.alloc_sbuf_tensor("idn", [128, 128], F32)
    onesb = nc.alloc_sbuf_tensor("onesb", [128, 1], BF16)
    selbb = nc.alloc_sbuf_tensor("selbb", [8, 1024], BF16)
    onesrf = nc.alloc_sbuf_tensor("onesrf", [1, 128], F32)
    msk = nc.alloc_sbuf_tensor("msk", [128, 96], F32)

    psA = nc.alloc_psum_tensor("psA", [128, 2048], F32)
    psB = nc.alloc_psum_tensor("psB", [128, 2048], F32)

    class PBank:

        def __init__(self, t, off):
            self.t, self.off = t, off

        def __getitem__(self, idx):
            rs, cs = idx
            return self.t[rs, self.off + (cs.start or 0): self.off + cs.stop]

    ps = ([PBank(psA, i * 512) for i in range(4)]
          + [PBank(psB, i * 512) for i in range(4)])

    S = Slot
    sl_x = [[S() for _ in range(NTT)] for _ in range(NC)]
    sl_xmm = [[S() for _ in range(NTT)] for _ in range(NC)]
    sl_ps = [S() for _ in range(8)]
    sl_rows = [S() for _ in range(4)]
    sl_m2 = S()
    sl_cs1 = [S() for _ in range(8)]
    sl_mfm = [S() for _ in range(T // 128)]
    sl_msk = [S(), S()]
    sl_stg = [[S() for _ in range(4)] for _ in range(2)]
    sl_htmp = [S() for _ in range(4)]
    sl_gh = [S() for _ in range(8)]
    sl_z = [S() for _ in range(NC)]
    sl_scn = [S(), S()]
    sl_cols = S()
    sl_small = S()
    sl_wgtb = S()
    sl_csgb = S()
    sl_idnb = S()
    sl_lgsb = S()
    sl_b1 = S()
    sl_cary = [S() for _ in range(8)]
    sl_out = S()
    sl_wffn1 = [S() for _ in range(4)]
    sl_wffn2 = [S() for _ in range(4)]
    sl_wgc = [S() for _ in range(4)]
    sl_wsw = [S() for _ in range(SWG * 3)]
    sl_w2s = [S() for _ in range(8)]

    CD = 1.0 / float(D)
    R0, R1, R2S, R3 = 0, TT, 2 * TT, 3 * TT

    def xcol(c, t0):
        return c * T + t0

    def setup():
        sch.emit("d", lambda e: e.dma_start(idn[:, :], ident[:, :]),
                 writes=[sl_small], inc=16)
        sch.emit("d", lambda e: e.dma_start(cols[:, 0:176], pk_t[:, :]),
                 writes=[sl_cols], inc=16)
        sch.emit("d", lambda e: e.dma_start(wgtf[:, :], wgt_t[:, :]),
                 writes=[sl_wgtb], inc=16)
        sch.emit("v", lambda e: e.memset(onesb[:, :], 1.0), writes=[sl_small])
        sch.emit("v", lambda e: e.memset(onesrf[0:1, :], 1.0), writes=[sl_small])
        sch.emit("v", lambda e: e.memset(cols[:, 176:177], 1.0), writes=[sl_cols])
        sch.emit("v", lambda e: e.memset(msk[:, 24:25], 1.0), writes=[sl_msk[0]])
        sch.emit("v", lambda e: e.memset(msk[:, 72:73], 1.0), writes=[sl_msk[1]])
        sch.emit("v", lambda e: e.tensor_scalar_mul(cols[:, 8:16], cols[:, 8:16], -1.0),
                 reads=[sl_cols], writes=[sl_cols])
        sch.emit("s", lambda e: e.activation(wgtb[:, :], wgtf[:, :], AF.Copy),
                 reads=[sl_wgtb], writes=[sl_wgtb])
        sch.emit("d", lambda e: e.dma_start(
            bass.AP(stg, 1024, [[2048, 8], [1, D]]),
            bass.AP(f_b2, 0, [[D, 8], [1, D]])),
            writes=sl_stg[1], inc=16)
        sch.emit("s", lambda e: e.activation(b2b[0:8, :], stg[0:8, 1024:2048], AF.Copy),
                 reads=sl_stg[1], writes=[sl_small])
        sch.emit("d", lambda e: e.dma_start(
            bass.AP(stg, 1024, [[2048, 8], [1, 1024]]),
            bass.AP(selb, 0, [[1024, 8], [1, 1024]])),
            writes=sl_stg[1], inc=16)
        sch.emit("s", lambda e: e.activation(selbb[0:8, :], stg[0:8, 1024:2048], AF.Copy),
                 reads=sl_stg[1], writes=[sl_small])

        def mm_csg(e):
            for ci in range(8):
                mm = e.matmul(ps[7][0:1, 0:8], onesb[:, 0:1],
                              wgtb[:, ci * 8:(ci + 1) * 8],
                              start=(ci == 0), stop=(ci == 7))
            return mm
        sch.emit("t", mm_csg, reads=[sl_small, sl_wgtb], writes=[sl_ps[7]])
        sch.emit("s", lambda e: e.activation(csgb[0:1, 0:8], ps[7][0:1, 0:8], AF.Copy),
                 reads=[sl_ps[7]], writes=[sl_csgb])

    def load_x(seq):
        for c in range(NC):
            for hv in range(2):
                sch.emit("d" if (c + hv) % 2 == 0 else "g",
                         lambda e, c=c, hv=hv: e.dma_start(
                    bass.AP(xr, xcol(c, hv * 1024), [[NC * T, 128], [1, 1024]]),
                    bass.AP(xin, seq * D * T + c * 128 * T + hv * 1024,
                            [[T, 128], [1, 1024]])),
                    writes=[sl_x[c][hv * 2], sl_x[c][hv * 2 + 1]], inc=16)

    def store_x(seq):
        for c in range(NC):
            for hv in range(2):
                sch.emit("d" if (c + hv) % 2 == 0 else "g",
                         lambda e, seq=seq, c=c, hv=hv: e.dma_start(
                    bass.AP(yout, seq * D * T + c * 128 * T + hv * 1024,
                            [[T, 128], [1, 1024]]),
                    bass.AP(xr, xcol(c, hv * 1024), [[NC * T, 128], [1, 1024]])),
                    reads=[sl_x[c][hv * 2], sl_x[c][hv * 2 + 1]], writes=[sl_out],
                    inc=16, sem="o")

    def ln_ffn_xmm(sl):
        for lt in range(2):
            nt = sl * 2 + lt
            t0 = nt * TT
            pb_s1, pb_s2, pb_r = 4, 5, 6

            def mm_sum(e, t0=t0):
                for c in range(NC):
                    mm = e.matmul(ps[pb_s1][0:1, 0:TT], cols[:, 176:177],
                                  xr[:, xcol(c, t0): xcol(c, t0) + TT],
                                  start=(c == 0), stop=(c == NC - 1))
                return mm
            sch.emit("t", mm_sum, reads=[sl_cols] + [sl_x[c][nt] for c in range(NC)],
                     writes=[sl_ps[pb_s1]])
            for c in range(NC):
                hb = c % 2
                sch.emit("s", lambda e, c=c, t0=t0, hb=hb: e.activation(
                    htmp[:, hb * TT:(hb + 1) * TT],
                    xr[:, xcol(c, t0): xcol(c, t0) + TT], AF.Square),
                    reads=[sl_x[c][nt]], writes=[sl_htmp[hb]])
                sch.emit("t", lambda e, c=c, hb=hb: e.matmul(
                    ps[pb_s2][0:1, 0:TT], onesb[:, 0:1],
                    htmp[:, hb * TT:(hb + 1) * TT],
                    start=(c == 0), stop=(c == NC - 1)),
                    reads=[sl_htmp[hb], sl_small], writes=[sl_ps[pb_s2]])
            sch.emit("v", lambda e: e.tensor_scalar_mul(
                rows[0:1, R0:R0 + TT], ps[pb_s1][0:1, 0:TT], CD),
                reads=[sl_ps[pb_s1]], writes=[sl_rows[0]])
            sch.emit("v", lambda e: e.tensor_mul(
                rows[0:1, R3:R3 + TT], rows[0:1, R0:R0 + TT], rows[0:1, R0:R0 + TT]),
                reads=[sl_rows[0]], writes=[sl_rows[3]])
            sch.emit("v", lambda e: e.scalar_tensor_tensor(
                rows[0:1, R1:R1 + TT], ps[pb_s2][0:1, 0:TT], CD,
                rows[0:1, R3:R3 + TT], ALU.mult, ALU.subtract),
                reads=[sl_ps[pb_s2], sl_rows[3]], writes=[sl_rows[1]])
            sch.emit("v", lambda e: e.tensor_scalar(
                rows[0:1, R3:R3 + TT], rows[0:1, R1:R1 + TT],
                1.0 + LN_EPS, LN_EPS * LN_EPS, ALU.mult, ALU.add),
                reads=[sl_rows[1]], writes=[sl_rows[3]])
            sch.emit("v", lambda e: e.reciprocal(
                rows[0:1, R3:R3 + TT], rows[0:1, R3:R3 + TT]),
                reads=[sl_rows[3]], writes=[sl_rows[3]])
            sch.emit("s", lambda e: e.activation(
                rows[0:1, R2S:R2S + TT], rows[0:1, R3:R3 + TT], AF.Sqrt),
                reads=[sl_rows[3]], writes=[sl_rows[2]])
            sch.emit("v", lambda e, t0=t0: e.scalar_tensor_tensor(
                m2row[0:1, t0:t0 + TT], rows[0:1, R0:R0 + TT], -1.0,
                rows[0:1, R2S:R2S + TT], ALU.mult, ALU.mult),
                reads=[sl_rows[0], sl_rows[2]], writes=[sl_m2])
            sch.emit("t", lambda e: e.matmul(
                ps[pb_r][:, 0:TT], onesrf[0:1, :], rows[0:1, R2S:R2S + TT],
                start=True, stop=True),
                reads=[sl_rows[2], sl_small], writes=[sl_ps[pb_r]])
            for c in range(NC):
                sch.emit("v", lambda e, c=c, t0=t0: e.tensor_mul(
                    xmm[:, xcol(c, t0): xcol(c, t0) + TT],
                    xr[:, xcol(c, t0): xcol(c, t0) + TT], ps[pb_r][:, 0:TT]),
                    reads=[sl_x[c][nt], sl_ps[pb_r]], writes=[sl_xmm[c][nt]])
            router(nt)

    def router(nt):
        t0 = nt * TT
        pb, pb2 = 7, 6

        def mm_rt(e, t0=t0):
            for c in range(NC):
                e.matmul(ps[pb][0:8, 0:TT], wgtb[:, c * 8:(c + 1) * 8],
                         xmm[:, xcol(c, t0): xcol(c, t0) + TT],
                         start=(c == 0), stop=False)
            return e.matmul(ps[pb][0:8, 0:TT], csgb[0:1, 0:8],
                            m2row[0:1, t0:t0 + TT], start=False, stop=True)
        sch.emit("t", mm_rt,
                 reads=[sl_xmm[c][nt] for c in range(NC)] + [sl_wgtb, sl_csgb, sl_m2],
                 writes=[sl_ps[pb]])
        sch.emit("s", lambda e: e.activation(lgsb[0:8, 0:TT], ps[pb][0:8, 0:TT], AF.Copy),
                 reads=[sl_ps[pb]], writes=[sl_lgsb])
        for sub in range(4):
            i = nt * 4 + sub
            tg = t0 + sub * 128
            pb2 = 3 if sub % 2 == 0 else 2
            mo = 48 * (sub % 2)
            smk = sl_msk[sub % 2]
            sch.emit("t", lambda e, sub=sub, pb2=pb2: e.transpose(
                ps[pb2][0:128, 0:8], lgsb[0:8, sub * 128:(sub + 1) * 128],
                idn[0:8, 0:8]),
                reads=[sl_lgsb, sl_small], writes=[sl_ps[pb2]])
            sch.emit("s", lambda e, pb2=pb2, mo=mo: e.activation(
                msk[:, mo:mo + 8], ps[pb2][:, 0:8], AF.Copy),
                reads=[sl_ps[pb2]], writes=[smk])
            sch.emit("v", lambda e, mo=mo: e.reduce_max(
                msk[:, mo + 46:mo + 47], msk[:, mo:mo + 8], axis=AX.X),
                reads=[smk], writes=[smk])
            sch.emit("v", lambda e, mo=mo: e.tensor_single_scalar(
                msk[:, mo + 8:mo + 16], msk[:, mo:mo + 8],
                msk[:, mo + 46:mo + 47], ALU.is_ge),
                reads=[smk], writes=[smk])
            sch.emit("v", lambda e, mo=mo: e.tensor_scalar(
                msk[:, mo + 16:mo + 24], msk[:, mo + 8:mo + 16],
                -1.0, 1.0, ALU.mult, ALU.add),
                reads=[smk], writes=[smk])
            sch.emit("v", lambda e, mo=mo: e.tensor_tensor_scan(
                msk[:, mo + 25:mo + 33], msk[:, mo + 16:mo + 24],
                msk[:, mo + 16:mo + 24], 1.0, ALU.mult, ALU.bypass),
                reads=[smk], writes=[smk])
            sch.emit("v", lambda e, mo=mo: e.tensor_mul(
                msk[:, mo + 33:mo + 41], msk[:, mo + 8:mo + 16],
                msk[:, mo + 24:mo + 32]),
                reads=[smk], writes=[smk])
            sch.emit("t", lambda e, pb2=pb2, mo=mo: e.transpose(
                ps[pb2][0:8, 128:256], msk[:, mo + 33:mo + 41], idn[:, :]),
                reads=[smk, sl_small], writes=[sl_ps[pb2]])
            sch.emit("s", lambda e, tg=tg, pb2=pb2: e.activation(
                mfm[0:8, tg:tg + 128], ps[pb2][0:8, 128:256], AF.Copy),
                reads=[sl_ps[pb2]], writes=[sl_mfm[i]])

    def ffn(first_call):
        hand_off([sl_wffn1[0]], [sl_wsw[0], sl_wsw[1], sl_wgc[0], sl_wgc[1]])
        hand_off([sl_wffn2[0]], [sl_wsw[3], sl_wsw[4], sl_wgc[2], sl_wgc[3]])
        hand_off([sl_wffn1[1]], [sl_wsw[6], sl_wsw[7]])
        hand_off([sl_wffn2[1]], [sl_wsw[9], sl_wsw[10]])
        hand_off([sl_wffn1[2]], [sl_w2s[0], sl_w2s[1]])
        hand_off([sl_wffn2[2]], [sl_w2s[2], sl_w2s[3]])
        hand_off([sl_wffn1[3]], [sl_w2s[4], sl_w2s[5]])
        hand_off([sl_wffn2[3]], [sl_w2s[6], sl_w2s[7]])
        hand_off(sl_wffn1, [sl_wsw[2], sl_wsw[5], sl_wsw[8], sl_wsw[11]])
        hand_off(sl_gh, sl_z)
        hand_off([sl_lgsb], [sl_scn[0]])
        for sl in range(NSL):
            ln_ffn_xmm(sl)
        def stream_e(e_):
            slot = e_ % 4
            w1c = slot * 4096
            w2c = slot * 4096 + 2048
            for cih in range(2):
                for c4 in range(4):
                    sch.emit("d", lambda e, e2=e_, cih=cih, c4=c4: e.dma_start(
                        bass.AP(stg, c4 * 256, [[2048, 128], [1, 256]]),
                        bass.AP(f_w1, (e2 * 2 + cih) * 128 * D + c4 * 256,
                                [[D, 128], [1, 256]])),
                        writes=[sl_stg[0][c4]], inc=16)
                for c4 in range(4):
                    ci = cih * 4 + c4
                    sch.emit("s", lambda e, c4=c4, e2=e_, ci=ci, w1c=w1c: e.activation(
                        wa[:, w1c + ci * 256: w1c + ci * 256 + 256],
                        stg[:, c4 * 256:(c4 + 1) * 256],
                        AF.Copy, scale=cols[:, 32 + e2 * 8 + ci:33 + e2 * 8 + ci]),
                        reads=[sl_stg[0][c4], sl_cols], writes=[sl_wffn1[slot]])
                if first_call:
                    for hh in range(2):
                        pbb = 5 + hh

                        def mm_b1(e, e2=e_, hh=hh, cih=cih, pbb=pbb):
                            for c4 in range(4):
                                ci = cih * 4 + c4
                                mm = e.matmul(
                                    ps[pbb][:, 0:1],
                                    stg[:, c4 * 256 + hh * 128: c4 * 256 + hh * 128 + 128],
                                    cols[:, 96 + e2 * 8 + ci:97 + e2 * 8 + ci],
                                    start=(ci == 0), stop=(ci == 7))
                            return mm
                        sch.emit("t", mm_b1, reads=sl_stg[0] + [sl_cols],
                                 writes=[sl_ps[pbb]])
                        if cih == 1:
                            sch.emit("v", lambda e, e2=e_, hh=hh, pbb=pbb: e.tensor_add(
                                bias1[:, e2 * 2 + hh:e2 * 2 + hh + 1],
                                ps[pbb][:, 0:1],
                                cols[:, 160 + e2 * 2 + hh:161 + e2 * 2 + hh]),
                                reads=[sl_ps[pbb], sl_cols], writes=[sl_b1])
        def stream_w2(e_):
            slot = e_ % 4
            w2c = slot * 4096 + 2048
            for hh in range(2):
                for c4 in range(4):
                    sch.emit("g", lambda e, e2=e_, hh=hh, c4=c4: e.dma_start(
                        bass.AP(stg, 1024 + c4 * 256, [[2048, 128], [1, 256]]),
                        bass.AP(f_w2, (e2 * EH + hh * 128) * D + c4 * 256,
                                [[D, 128], [1, 256]])),
                        writes=[sl_stg[1][c4]], inc=16)
                sch.emit("s", lambda e, wcol=w2c + hh * 1024: e.activation(
                    wa[:, wcol:wcol + 1024], stg[:, 1024:2048], AF.Copy),
                    reads=sl_stg[1], writes=[sl_wffn2[slot]])

        def stage_e(e_):
            slot = e_ % 4
            w1c = slot * 4096
            row0 = slot * 2
            for hh in range(2):
                pbb = 4

                def mm_cs(e, w1c=w1c, hh=hh):
                    for ci in range(8):
                        wcol = w1c + ci * 256 + hh * 128
                        mm = e.matmul(ps[pbb][0:1, 0:128], onesb[:, :],
                                      wa[:, wcol:wcol + 128],
                                      start=(ci == 0), stop=(ci == 7))
                    return mm
                sch.emit("t", mm_cs, reads=[sl_wffn1[slot], sl_small],
                         writes=[sl_ps[pbb]])
                sch.emit("v", lambda e, e2=e_, hh=hh: e.tensor_copy(
                    cs1[0:1, (e2 * 2 + hh) * 128:(e2 * 2 + hh) * 128 + 128],
                    ps[pbb][0:1, 0:128]),
                    reads=[sl_ps[pbb]], writes=[sl_cs1[slot * 2 + hh]])
            for nt in range(NTT):
                sch.emit("t", lambda e, e2=e_, nt=nt: e.matmul(
                    ps[4 + nt][:, 0:TT], selbb[0:8, e2 * 128:(e2 + 1) * 128],
                    mfm[0:8, nt * TT:nt * TT + TT], start=True, stop=True),
                    reads=[sl_mfm[nt * 4 + j] for j in range(4)] + [sl_small],
                    writes=[sl_ps[4 + nt]])
            for hh in range(2):
                def mm_s1(e, e2=e_, hh=hh, w1c=w1c):
                    for ci in range(8):
                        wcol = w1c + ci * 256 + hh * 128
                        for nt in range(NTT):
                            e.matmul(ps[nt][:, 0:TT], wa[:, wcol:wcol + 128],
                                     xmm[:, xcol(ci, nt * TT): xcol(ci, nt * TT) + TT],
                                     start=(ci == 0), stop=False)
                    for nt in range(NTT):
                        mm = e.matmul(
                            ps[nt][:, 0:TT],
                            cs1[0:1, (e2 * 2 + hh) * 128:(e2 * 2 + hh) * 128 + 128],
                            m2row[0:1, nt * TT:nt * TT + TT],
                            start=False, stop=True)
                    return mm
                rd = [sl_wffn1[slot], sl_cs1[slot * 2 + hh], sl_m2]
                for nt in range(NTT):
                    rd += [sl_xmm[c][nt] for c in range(NC)]
                sch.emit("t", mm_s1, reads=rd,
                         writes=[sl_ps[nt] for nt in range(NTT)])
                sch.emit("s", lambda e, e2=e_, hh=hh: e.activation(
                    htmp[:, 0:T], psA[:, 0:T], AF.Gelu,
                    bias=bias1[:, e2 * 2 + hh:e2 * 2 + hh + 1]),
                    reads=[sl_ps[0], sl_ps[1], sl_ps[2], sl_ps[3], sl_b1],
                    writes=sl_htmp)
                gcol = (row0 + hh) * T
                sch.emit("v", lambda e, gcol=gcol: e.tensor_mul(
                    ghb[:, gcol: gcol + T], htmp[:, 0:T], psB[:, 0:T]),
                    reads=sl_htmp + [sl_ps[4], sl_ps[5], sl_ps[6], sl_ps[7]],
                    writes=[sl_gh[row0 + hh]])

        def y_half(half):
            for dch in range(NC):
                yb = [0, 1, 2, 3] if dch % 2 == 0 else [4, 5, 6, 7]

                def mm_y(e, dch=dch, yb=yb, half=half):
                    for slot in range(4):
                        for hh in range(2):
                            wcol = slot * 4096 + 2048 + hh * 1024 + dch * 128
                            grow = (slot * 2 + hh) * T
                            last1 = (slot == 3 and hh == 1 and half == 0)
                            for nt in range(NTT):
                                mm = e.matmul(
                                    ps[yb[nt]][:, 0:TT], wa[:, wcol:wcol + 128],
                                    ghb[:, grow + nt * TT: grow + nt * TT + TT],
                                    start=(slot == 0 and hh == 0), stop=last1)
                    if half == 1:
                        for nt in range(NTT):
                            mm = e.matmul(ps[yb[nt]][:, 0:TT],
                                          b2b[0:8, dch * 128:dch * 128 + 128],
                                          mfm[0:8, nt * TT:nt * TT + TT],
                                          start=False, stop=True)
                    return mm
                rd = [sl_wffn2[s2] for s2 in range(4)] + [sl_gh[g] for g in range(8)]
                if half == 1:
                    rd = rd + [sl_small] + sl_mfm
                sch.emit("t", mm_y, reads=rd, writes=[sl_ps[b] for b in yb])
                pst = psA if dch % 2 == 0 else psB
                for pr in range(2):
                    sch.emit("v", lambda e, dch=dch, pr=pr, pst=pst: e.scalar_tensor_tensor(
                        xr[:, xcol(dch, pr * 1024): xcol(dch, pr * 1024) + 1024],
                        pst[:, pr * 1024:(pr + 1) * 1024], 0.5,
                        xr[:, xcol(dch, pr * 1024): xcol(dch, pr * 1024) + 1024],
                        ALU.mult, ALU.add),
                        reads=[sl_ps[yb[2 * pr]], sl_ps[yb[2 * pr + 1]],
                               sl_x[dch][2 * pr], sl_x[dch][2 * pr + 1]],
                        writes=[sl_x[dch][2 * pr], sl_x[dch][2 * pr + 1]])

        stream_e(0)
        stream_w2(0)
        stream_e(1)
        stream_w2(1)
        for e_ in range(NE):
            if e_ + 2 < NE:
                stream_e(e_ + 2)
            if e_ + 2 < 4:
                stream_w2(e_ + 2)
            stage_e(e_)
            if e_ == 3:
                y_half(0)
                stream_w2(4)
                stream_w2(5)
            elif e_ in (4, 5):
                stream_w2(e_ + 2)
        y_half(1)

    def slice_stats(src, src_slots, col_of, nt_local, with_sum, pb_s1, pb_s2):
        if with_sum:
            def mm_sum(e, nt_local=nt_local):
                for c in range(NC):
                    off = col_of(c, nt_local * TT)
                    mm = e.matmul(ps[pb_s1][0:1, 0:TT], cols[:, 176:177],
                                  src[:, off:off + TT],
                                  start=(c == 0), stop=(c == NC - 1))
                return mm
            sch.emit("t", mm_sum, reads=[sl_cols] + src_slots,
                     writes=[sl_ps[pb_s1]])
        for c in range(NC):
            hb = c % 2
            off = col_of(c, nt_local * TT)
            sch.emit("s", lambda e, off=off, hb=hb: e.activation(
                htmp[:, hb * TT:(hb + 1) * TT], src[:, off:off + TT], AF.Square),
                reads=src_slots, writes=[sl_htmp[hb]])
            sch.emit("t", lambda e, c=c, hb=hb: e.matmul(
                ps[pb_s2][0:1, 0:TT], onesb[:, 0:1], htmp[:, hb * TT:(hb + 1) * TT],
                start=(c == 0), stop=(c == NC - 1)),
                reads=[sl_htmp[hb], sl_small], writes=[sl_ps[pb_s2]])

    def mixer(seq_unused):
        hand_off(sl_z, sl_gh)
        hand_off([sl_scn[0]], [sl_lgsb])
        for H in range(T // HL):
            tb = H * HL
            ntg = [tb // TT, tb // TT + 1]
            for lt in range(HL // TT):
                nt = ntg[lt]
                pb_s1, pb_s2, pb_r, pb_m = 4, 5, 6, 7
                slice_stats(xr, [sl_x[c][nt] for c in range(NC)], xcol,
                            nt, True, pb_s1, pb_s2)
                sch.emit("v", lambda e: e.tensor_scalar_mul(
                    rows[0:1, R0:R0 + TT], ps[pb_s1][0:1, 0:TT], CD),
                    reads=[sl_ps[pb_s1]], writes=[sl_rows[0]])
                sch.emit("v", lambda e: e.tensor_mul(
                    rows[0:1, R3:R3 + TT], rows[0:1, R0:R0 + TT], rows[0:1, R0:R0 + TT]),
                    reads=[sl_rows[0]], writes=[sl_rows[3]])
                sch.emit("v", lambda e: e.scalar_tensor_tensor(
                    rows[0:1, R1:R1 + TT], ps[pb_s2][0:1, 0:TT], CD,
                    rows[0:1, R3:R3 + TT], ALU.mult, ALU.subtract),
                    reads=[sl_ps[pb_s2], sl_rows[3]], writes=[sl_rows[1]])
                sch.emit("v", lambda e: e.tensor_scalar_add(
                    rows[0:1, R2S:R2S + TT], rows[0:1, R1:R1 + TT], LN_EPS),
                    reads=[sl_rows[1]], writes=[sl_rows[2]])
                sch.emit("v", lambda e: e.reciprocal(
                    rows[0:1, R2S:R2S + TT], rows[0:1, R2S:R2S + TT]),
                    reads=[sl_rows[2]], writes=[sl_rows[2]])
                sch.emit("s", lambda e: e.activation(
                    rows[0:1, R2S:R2S + TT], rows[0:1, R2S:R2S + TT], AF.Sqrt),
                    reads=[sl_rows[2]], writes=[sl_rows[2]])
                sch.emit("v", lambda e: e.scalar_tensor_tensor(
                    rows[0:1, R3:R3 + TT], rows[0:1, R0:R0 + TT], -1.0,
                    rows[0:1, R2S:R2S + TT], ALU.mult, ALU.mult),
                    reads=[sl_rows[0], sl_rows[2]], writes=[sl_rows[3]])
                sch.emit("t", lambda e: e.matmul(
                    ps[pb_r][:, 0:TT], onesrf[0:1, :], rows[0:1, R2S:R2S + TT],
                    start=True, stop=True),
                    reads=[sl_rows[2], sl_small], writes=[sl_ps[pb_r]])
                sch.emit("t", lambda e: e.matmul(
                    ps[pb_m][:, 0:TT], onesrf[0:1, :], rows[0:1, R3:R3 + TT],
                    start=True, stop=True),
                    reads=[sl_rows[3], sl_small], writes=[sl_ps[pb_m]])
                sch.emit("v", lambda e: e.tensor_mul(
                    rows[0:1, R0:R0 + TT], rows[0:1, R2S:R2S + TT],
                    rows[0:1, R2S:R2S + TT]),
                    reads=[sl_rows[2]], writes=[sl_rows[0]])
                sch.emit("v", lambda e: e.tensor_mul(
                    rows[0:1, R0:R0 + TT], rows[0:1, R0:R0 + TT],
                    rows[0:1, R1:R1 + TT]),
                    reads=[sl_rows[0], sl_rows[1]], writes=[sl_rows[0]])
                sch.emit("v", lambda e: e.tensor_scalar_add(
                    rows[0:1, R0:R0 + TT], rows[0:1, R0:R0 + TT], RMS_EPS),
                    reads=[sl_rows[0]], writes=[sl_rows[0]])
                sch.emit("v", lambda e: e.reciprocal(
                    rows[0:1, R0:R0 + TT], rows[0:1, R0:R0 + TT]),
                    reads=[sl_rows[0]], writes=[sl_rows[0]])
                sch.emit("s", lambda e: e.activation(
                    rows[0:1, R0:R0 + TT], rows[0:1, R0:R0 + TT], AF.Sqrt),
                    reads=[sl_rows[0]], writes=[sl_rows[0]])
                sch.emit("t", lambda e: e.matmul(
                    ps[4][:, 0:TT], onesrf[0:1, :], rows[0:1, R0:R0 + TT],
                    start=True, stop=True),
                    reads=[sl_rows[0], sl_small], writes=[sl_ps[4]])
                for c in range(NC):
                    zo = c * HL + lt * TT
                    sch.emit("v", lambda e, c=c, nt=nt, zo=zo: e.tensor_mul(
                        zb[:, zo:zo + TT],
                        xr[:, xcol(c, nt * TT): xcol(c, nt * TT) + TT],
                        ps[pb_r][:, 0:TT]),
                        reads=[sl_x[c][nt], sl_ps[pb_r]], writes=[sl_z[c]])
                    sch.emit("v", lambda e, zo=zo: e.tensor_add(
                        zb[:, zo:zo + TT], zb[:, zo:zo + TT], ps[pb_m][:, 0:TT]),
                        reads=[sl_z[c], sl_ps[pb_m]], writes=[sl_z[c]])
                for c in range(NC):
                    zo = c * HL + lt * TT
                    xo = xcol(c, tb + lt * TT)
                    sch.emit("v", lambda e, c=c, zo=zo, xo=xo: e.scalar_tensor_tensor(
                        xmm[:, xo:xo + TT],
                        zb[:, zo:zo + TT], cols[:, 16 + c:17 + c], ps[4][:, 0:TT],
                        ALU.mult, ALU.mult),
                        reads=[sl_z[c], sl_ps[4], sl_cols], writes=[sl_xmm[c][nt]])
            hand_off(sl_wgc, sl_wffn1 + sl_wffn2 + sl_wsw + sl_w2s)
            for f in range(NC):
                buf = f % 2
                for m, w in enumerate((a_wg, a_wc)):
                    for c4 in range(4):
                        sch.emit("d" if m == 0 else "g",
                                 lambda e, w=w, f=f, m=m, c4=c4: e.dma_start(
                            bass.AP(stg, m * 1024 + c4 * 256, [[2048, 128], [1, 256]]),
                            bass.AP(w, f * 128 * D + c4 * 256, [[D, 128], [1, 256]])),
                            writes=[sl_stg[m][c4]], inc=16)
                    wcol = buf * 2048 + m * 1024
                    sch.emit("s", lambda e, m=m, wcol=wcol: e.activation(
                        wa[:, wcol:wcol + 1024], stg[:, m * 1024:(m + 1) * 1024],
                        AF.Copy),
                        reads=sl_stg[m], writes=[sl_wgc[buf * 2 + m]])
                for m in range(2):
                    wcol = buf * 2048 + m * 1024
                    wslot = sl_wgc[buf * 2 + m]
                    pre = [0 + m * 2, 1 + m * 2]

                    def mm_gc(e, wcol=wcol, pre=pre, tb=tb):
                        for ci in range(8):
                            for k in range(2):
                                mm = e.matmul(
                                    ps[pre[k]][:, 0:TT],
                                    wa[:, wcol + ci * 128:wcol + ci * 128 + 128],
                                    xmm[:, xcol(ci, tb + k * TT): xcol(ci, tb + k * TT) + TT],
                                    start=(ci == 0), stop=(ci == 7))
                        return mm
                    rd = [wslot]
                    for k in range(2):
                        rd += [sl_xmm[c][ntg[k]] for c in range(NC)]
                    sch.emit("t", mm_gc, reads=rd,
                             writes=[sl_ps[pre[0]], sl_ps[pre[1]]])
                    for k in range(2):
                        if m == 0:
                            sch.emit("s", lambda e, f=f, k=k, pb=pre[k]: e.activation(
                                sc2[:, k * TT:k * TT + TT], ps[pb][:, 0:TT],
                                AF.Sigmoid, bias=cols[:, f:f + 1]),
                                reads=[sl_ps[pre[k]], sl_cols], writes=[sl_scn[0]])
                        else:
                            sch.emit("s", lambda e, f=f, k=k, pb=pre[k]: e.activation(
                                sc2[:, HL + k * TT: HL + k * TT + TT], ps[pb][:, 0:TT],
                                AF.Tanh, bias=cols[:, 8 + f:9 + f], scale=-1.0),
                                reads=[sl_ps[pre[k]], sl_cols],
                                writes=[sl_scn[1]])
                sch.emit("v", lambda e: e.scalar_tensor_tensor(
                    sc2[:, HL:2 * HL], sc2[:, 0:HL], 1.0,
                    sc2[:, HL:2 * HL], ALU.subtract, ALU.mult),
                    reads=[sl_scn[0], sl_scn[1]],
                    writes=[sl_scn[1]])
                if H == 0:
                    sch.emit("v", lambda e: e.tensor_tensor_scan(
                        sc2[:, HL:2 * HL], sc2[:, 0:HL],
                        sc2[:, HL:2 * HL], 0.0, ALU.mult, ALU.add),
                        reads=[sl_scn[0], sl_scn[1]],
                        writes=[sl_scn[1]])
                else:
                    sch.emit("v", lambda e, f=f: e.tensor_tensor_scan(
                        sc2[:, HL:2 * HL], sc2[:, 0:HL],
                        sc2[:, HL:2 * HL], cary[:, f:f + 1], ALU.mult, ALU.add),
                        reads=[sl_scn[0], sl_scn[1], sl_cary[f]],
                        writes=[sl_scn[1]])
                if H == 0:
                    sch.emit("v", lambda e, f=f: e.tensor_copy(
                        cary[:, f:f + 1], sc2[:, 2 * HL - 1:2 * HL]),
                        reads=[sl_scn[1]], writes=[sl_cary[f]])
                sch.emit("v", lambda e, f=f: e.tensor_add(
                    zb[:, f * HL:(f + 1) * HL], zb[:, f * HL:(f + 1) * HL],
                    sc2[:, HL:2 * HL]),
                    reads=[sl_scn[1], sl_z[f]], writes=[sl_z[f]])
            for lt in range(HL // TT):
                nt = ntg[lt]
                pb_s2, pb_r = 5, 6
                slice_stats(zb, [sl_z[c] for c in range(NC)],
                            lambda c, t: c * HL + lt * TT, 0, False, 4, pb_s2)
                sch.emit("v", lambda e: e.tensor_scalar_mul(
                    rows[0:1, R1:R1 + TT], ps[pb_s2][0:1, 0:TT], CD),
                    reads=[sl_ps[pb_s2]], writes=[sl_rows[1]])
                sch.emit("v", lambda e: e.tensor_scalar_add(
                    rows[0:1, R2S:R2S + TT], rows[0:1, R1:R1 + TT], RMS_EPS),
                    reads=[sl_rows[1]], writes=[sl_rows[2]])
                sch.emit("v", lambda e: e.reciprocal(
                    rows[0:1, R2S:R2S + TT], rows[0:1, R2S:R2S + TT]),
                    reads=[sl_rows[2]], writes=[sl_rows[2]])
                sch.emit("s", lambda e: e.activation(
                    rows[0:1, R2S:R2S + TT], rows[0:1, R2S:R2S + TT], AF.Sqrt),
                    reads=[sl_rows[2]], writes=[sl_rows[2]])
                sch.emit("t", lambda e: e.matmul(
                    ps[pb_r][:, 0:TT], onesrf[0:1, :], rows[0:1, R2S:R2S + TT],
                    start=True, stop=True),
                    reads=[sl_rows[2], sl_small], writes=[sl_ps[pb_r]])
                for c in range(NC):
                    zo = c * HL + lt * TT
                    xo = xcol(c, tb + lt * TT)
                    sch.emit("v", lambda e, c=c, zo=zo, xo=xo: e.scalar_tensor_tensor(
                        xmm[:, xo:xo + TT],
                        zb[:, zo:zo + TT], cols[:, 24 + c:25 + c], ps[pb_r][:, 0:TT],
                        ALU.mult, ALU.mult),
                        reads=[sl_z[c], sl_ps[pb_r], sl_cols], writes=[sl_xmm[c][nt]])
            for c in range(NC):
                for lt in range(HL // TT):
                    nt = ntg[lt]
                    zo = c * HL + lt * TT
                    sch.emit("v" if c % 2 == 0 else "g",
                             lambda e, c=c, nt=nt, zo=zo: e.tensor_add(
                        xr[:, xcol(c, nt * TT): xcol(c, nt * TT) + TT],
                        xr[:, xcol(c, nt * TT): xcol(c, nt * TT) + TT],
                        zb[:, zo:zo + TT]),
                        reads=[sl_z[c], sl_x[c][nt]], writes=[sl_x[c][nt]])
        hand_off(sl_wsw + sl_w2s, sl_wgc + sl_wffn1 + sl_wffn2)
        hand_off(sl_gh, sl_z)
        nj = DFF // 128
        for g in range(nj // 4):
            grp = list(range(g * 4, g * 4 + 4))
            for j in grp:
                slot = j % 4
                wbase = slot * 2048
                w2base = 8192 + (g % 2) * 4096 + slot * 1024
                for m, w in enumerate((a_w1, a_w3)):
                    for c4 in range(4):
                        sch.emit("d" if m == 0 else "g",
                                 lambda e, w=w, j=j, m=m, c4=c4: e.dma_start(
                            bass.AP(stg, m * 1024 + c4 * 256, [[2048, 128], [1, 256]]),
                            bass.AP(w, j * 128 * D + c4 * 256, [[D, 128], [1, 256]])),
                            writes=[sl_stg[m][c4]], inc=16)
                    sch.emit("s", lambda e, wbase=wbase, m=m: e.activation(
                        wa[:, wbase + m * 1024: wbase + m * 1024 + 1024],
                        stg[:, m * 1024:(m + 1) * 1024], AF.Copy),
                        reads=sl_stg[m], writes=[sl_wsw[slot * 3 + m]])
                for c4 in range(4):
                    sch.emit("d" if (j + c4) % 2 == 0 else "g",
                             lambda e, j=j, c4=c4: e.dma_start(
                        bass.AP(stg, c4 * 256, [[2048, 128], [1, 256]]),
                        bass.AP(a_w2, j * 128 * D + c4 * 256, [[D, 128], [1, 256]])),
                        writes=[sl_stg[0][c4]], inc=16)
                sch.emit("s", lambda e, w2base=w2base: e.activation(
                    wa[:, w2base: w2base + 1024],
                    stg[:, 0:1024], AF.Copy),
                    reads=sl_stg[0], writes=[sl_w2s[(g % 2) * 4 + slot]])
            for j in grp:
                slot = j % 4
                wbase = slot * 2048
                grow = (g % 2) * 4 + slot
                pA = [0, 1, 2, 3]
                pB = [4, 5, 6, 7]

                def mm_pre(e, wofs, pbs):
                    for ci in range(8):
                        for nt in range(NTT):
                            mm = e.matmul(
                                ps[pbs[nt]][:, 0:TT],
                                wa[:, wofs + ci * 128: wofs + ci * 128 + 128],
                                xmm[:, xcol(ci, nt * TT): xcol(ci, nt * TT) + TT],
                                start=(ci == 0), stop=(ci == 7))
                    return mm
                rdx = []
                for nt in range(NTT):
                    rdx += [sl_xmm[c][nt] for c in range(NC)]
                sch.emit("t", lambda e, f_=mm_pre, wb=wbase, pA=pA: f_(e, wb, pA),
                         reads=[sl_wsw[slot * 3]] + rdx,
                         writes=[sl_ps[b] for b in pA])
                sch.emit("s", lambda e: e.activation(
                    htmp[:, 0:T], psA[:, 0:T], AF.Silu),
                    reads=[sl_ps[b] for b in pA], writes=sl_htmp)
                sch.emit("t", lambda e, f_=mm_pre, wb=wbase, pB=pB: f_(e, wb + 1024, pB),
                         reads=[sl_wsw[slot * 3 + 1]] + rdx,
                         writes=[sl_ps[b] for b in pB])
                sch.emit("v", lambda e, grow=grow: e.tensor_mul(
                    ghb[:, grow * T: grow * T + T], htmp[:, 0:T], psB[:, 0:T]),
                    reads=sl_htmp + [sl_ps[b] for b in pB], writes=[sl_gh[grow]])
            for dch in range(NC):
                yb = [0, 1, 2, 3] if dch % 2 == 0 else [4, 5, 6, 7]

                def mm_y(e, dch=dch, grp=grp, yb=yb, g=g):
                    for jj, j in enumerate(grp):
                        slot = j % 4
                        wofs = 8192 + (g % 2) * 4096 + slot * 1024 + dch * 128
                        grow = (g % 2) * 4 + slot
                        for nt in range(NTT):
                            mm = e.matmul(
                                ps[yb[nt]][:, 0:TT], wa[:, wofs:wofs + 128],
                                ghb[:, grow * T + nt * TT: grow * T + nt * TT + TT],
                                start=(jj == 0), stop=(jj == len(grp) - 1))
                    return mm
                sch.emit("t", mm_y,
                         reads=[sl_w2s[(g % 2) * 4 + j % 4] for j in grp] +
                               [sl_gh[(g % 2) * 4 + j % 4] for j in grp],
                         writes=[sl_ps[b] for b in yb])
                pst = psA if dch % 2 == 0 else psB
                for pr in range(2):
                    sch.emit("v", lambda e, dch=dch, pr=pr, pst=pst: e.tensor_add(
                        xr[:, xcol(dch, pr * 1024): xcol(dch, pr * 1024) + 1024],
                        xr[:, xcol(dch, pr * 1024): xcol(dch, pr * 1024) + 1024],
                        pst[:, pr * 1024:(pr + 1) * 1024]),
                        reads=[sl_ps[yb[2 * pr]], sl_ps[yb[2 * pr + 1]],
                               sl_x[dch][2 * pr], sl_x[dch][2 * pr + 1]],
                        writes=[sl_x[dch][2 * pr], sl_x[dch][2 * pr + 1]])

    setup()
    for seq in range(NSEQ):
        load_x(seq)
        if upto >= 1:
            ffn(first_call=(seq == 0))
        if upto >= 2:
            mixer(seq)
        if upto >= 4:
            ffn(first_call=False)
        store_x(seq)
    sch.final_output_waits("d")

    import contextlib
    with contextlib.ExitStack() as stack:
        sems = {s: stack.enter_context(nc.semaphore(s)) for s in ALL_SEMS}
        with nc.Block() as block:
            def mk(eng_key):
                def body(eng):
                    for waits, build, sem, inc in sch.prog[eng_key]:
                        for s, v in waits:
                            eng.wait_ge(sems[s], v)
                        if build is not None:
                            inst = build(eng)
                            if inc:
                                inst.then_inc(sems[sem], inc)
                return body
            block.sync(mk("d"))
            block.gpsimd(mk("g"))
            block.tensor(mk("t"))
            block.scalar(mk("s"))
            block.vector(mk("v"))
    return nc


def kernel(L0_attn_x, L0_attn_Wg, L0_attn_bg, L0_attn_Wc, L0_attn_bc,
           L0_attn_n1_w, L0_attn_n2_w, L0_attn_W1, L0_attn_W3, L0_attn_W2,
           L0_ffn_Wg, L0_ffn_ln_scale, L0_ffn_ln_bias, L0_ffn_W1, L0_ffn_b1,
           L0_ffn_W2, L0_ffn_b2):
    x = np.ascontiguousarray(np.asarray(L0_attn_x, dtype=np.float32))
    B = x.shape[0]
    ncores = 8
    per = B // ncores

    def f32(v):
        return np.asarray(v, np.float32)

    pk = np.empty((128, 176), np.float32)
    pk[:, 0:8] = f32(L0_attn_bg).reshape(8, 128).T
    pk[:, 8:16] = f32(L0_attn_bc).reshape(8, 128).T
    pk[:, 16:24] = f32(L0_attn_n1_w).reshape(8, 128).T
    pk[:, 24:32] = f32(L0_attn_n2_w).reshape(8, 128).T
    pk[:, 32:96] = f32(L0_ffn_ln_scale).reshape(8, 8, 128).transpose(2, 0, 1).reshape(128, 64)
    pk[:, 96:160] = f32(L0_ffn_ln_bias).reshape(8, 8, 128).transpose(2, 0, 1).reshape(128, 64)
    pk[:, 160:176] = f32(L0_ffn_b1).reshape(8, 2, 128).transpose(2, 0, 1).reshape(128, 16)
    wgt = np.ascontiguousarray(
        f32(L0_ffn_Wg).reshape(8, 8, 128).transpose(2, 1, 0).reshape(128, 64))
    def dchunks(w):
        w = f32(w)
        n = w.shape[1]
        return np.ascontiguousarray(
            w.reshape(8, 128, n // 128, 128).transpose(2, 1, 0, 3)
            .reshape(n // 128, 128, 8 * 128))

    common = {
        "a_wg": dchunks(L0_attn_Wg),
        "a_wc": dchunks(L0_attn_Wc),
        "a_w1": dchunks(L0_attn_W1),
        "a_w3": dchunks(L0_attn_W3),
        "a_w2": np.ascontiguousarray(L0_attn_W2, np.float32),
        "f_w1": np.ascontiguousarray(
            f32(L0_ffn_W1).reshape(8, 2, 4, 128, 256).transpose(0, 1, 3, 2, 4)
            .reshape(8, 2, 128, 1024)),
        "f_w2": np.ascontiguousarray(L0_ffn_W2, np.float32),
        "f_b2": np.ascontiguousarray(L0_ffn_b2, np.float32),
        "pk": pk,
        "wgt": wgt,
        "ident": np.eye(128, dtype=np.float32),
        "selb": np.kron(np.eye(8, dtype=np.float32), np.ones((1, 128), np.float32)),
    }
    nc = build_program()
    in_maps = []
    for c in range(ncores):
        m = dict(common)
        m["xin"] = np.ascontiguousarray(
            x[c * per:(c + 1) * per].transpose(0, 2, 1))
        in_maps.append(m)
    res = run_bass_kernel_spmd(nc, in_maps, core_ids=list(range(ncores)))
    return np.concatenate(
        [r["yout"].transpose(0, 2, 1) for r in res.results], axis=0)


# version 19 (final)
# speedup vs baseline: 1.0472x; 1.0472x over previous
import numpy as np
import concourse.bass as bass
import concourse.mybir as mybir
from concourse.bass_utils import run_bass_kernel_spmd

F32 = mybir.dt.float32
BF16 = mybir.dt.bfloat16
AF = mybir.ActivationFunctionType
ALU = mybir.AluOpType
AX = mybir.AxisListType

D = 1024
T = 2048
NSEQ = 2
NC = 8
DFF = 4096
NE = 8
EH = 256
TT = 512
NTT = T // TT
HL = 1024
LN_EPS = 1e-5
RMS_EPS = 1e-6
SWG = 4

NSL = T // HL


class Slot:
    __slots__ = ("w", "r")

    def __init__(self):
        self.w = None
        self.r = []


DMA_POOL = {"d": [f"sd{i}" for i in range(8)],
            "g": [f"sg{i}" for i in range(4)],
            "o": [f"so{i}" for i in range(2)]}
ALL_SEMS = (["st", "sv", "ss", "sgc"]
            + DMA_POOL["d"] + DMA_POOL["g"] + DMA_POOL["o"])


class Sched:
    def __init__(self):
        self.prog = {e: [] for e in ("t", "v", "s", "d", "g")}
        self.semval = {s: 0 for s in ALL_SEMS}
        self.wm = {e: {} for e in self.prog}
        self.rr = {k: 0 for k in DMA_POOL}

    def emit(self, eng, build, reads=(), writes=(), inc=1, sem=None):
        need = {}

        def add(tok):
            if tok is not None:
                s, v = tok
                if need.get(s, 0) < v:
                    need[s] = v

        for sl in reads:
            add(sl.w)
        for sl in writes:
            add(sl.w)
            for tok in sl.r:
                add(tok)
        is_dma = inc == 16
        if is_dma:
            key = "o" if sem == "o" else eng
            pool = DMA_POOL[key]
            s_ = pool[self.rr[key] % len(pool)]
            self.rr[key] += 1
            if self.semval[s_] > 0:
                add((s_, self.semval[s_]))
            sem = s_
        elif sem is None:
            sem = {"t": "st", "v": "sv", "s": "ss", "g": "sgc"}[eng]
        wm = self.wm[eng]
        waits = sorted((s, v) for s, v in need.items() if v > wm.get(s, 0))
        for s, v in need.items():
            if v > wm.get(s, 0):
                wm[s] = v
        self.semval[sem] += inc
        tok = (sem, self.semval[sem])
        self.prog[eng].append((waits, build, sem, inc))
        for sl in writes:
            sl.w = tok
            sl.r = []
        for sl in reads:
            sl.r.append(tok)
        return tok

    def final_output_waits(self, eng):
        for s_ in DMA_POOL["o"]:
            if self.semval[s_] > 0:
                self.prog[eng].append(([(s_, self.semval[s_])], None, None, 0))


def hand_off(dst_slots, src_slots):
    toks = []
    for s in src_slots:
        if s.w is not None:
            toks.append(s.w)
        toks += s.r
        s.w = None
        s.r = []
    for d in dst_slots:
        d.r = d.r + toks


def build_program(upto=9):
    nc = bass.Bass("TRN2", target_bir_lowering=False)
    sch = Sched()

    xin = nc.dram_tensor("xin", [NSEQ, D, T], F32, kind="ExternalInput")
    a_wg = nc.dram_tensor("a_wg", [NC, 128, D], F32, kind="ExternalInput")
    a_wc = nc.dram_tensor("a_wc", [NC, 128, D], F32, kind="ExternalInput")
    a_w1 = nc.dram_tensor("a_w1", [DFF // 128, 128, D], F32, kind="ExternalInput")
    a_w3 = nc.dram_tensor("a_w3", [DFF // 128, 128, D], F32, kind="ExternalInput")
    a_w2 = nc.dram_tensor("a_w2", [DFF, D], F32, kind="ExternalInput")
    f_w1 = nc.dram_tensor("f_w1", [NE, 2, 128, D], F32, kind="ExternalInput")
    f_w2 = nc.dram_tensor("f_w2", [NE, EH, D], F32, kind="ExternalInput")
    f_b2 = nc.dram_tensor("f_b2", [NE, D], F32, kind="ExternalInput")
    pk_t = nc.dram_tensor("pk", [128, 176], F32, kind="ExternalInput")
    wgt_t = nc.dram_tensor("wgt", [128, 64], F32, kind="ExternalInput")
    ident = nc.dram_tensor("ident", [128, 128], F32, kind="ExternalInput")
    selb = nc.dram_tensor("selb", [8, 1024], F32, kind="ExternalInput")
    yout = nc.dram_tensor("yout", [NSEQ, D, T], F32, kind="ExternalOutput")

    xr = nc.alloc_sbuf_tensor("xr", [128, NC * T], F32)
    xmm = nc.alloc_sbuf_tensor("xmm", [128, NC * T], BF16)
    wa = nc.alloc_sbuf_tensor("wa", [128, 16384], BF16)
    ghb = nc.alloc_sbuf_tensor("ghb", [128, 16384], BF16)
    zb = nc.alloc_sbuf_tensor_at("zb", [128, NC * HL], F32,
                                 offset=nc.lookup_mloc(ghb).addr)
    sc2 = nc.alloc_sbuf_tensor("sc2", [128, 2 * HL], F32)
    lgsb = nc.alloc_sbuf_tensor_at("lgsb", [8, TT], F32,
                                   offset=nc.lookup_mloc(sc2).addr)
    stg = nc.alloc_sbuf_tensor("stg", [128, 2048], F32)
    htmp = nc.alloc_sbuf_tensor("htmp", [128, 2048], BF16)
    rows = nc.alloc_sbuf_tensor("rows", [1, 4 * TT], F32)
    m2row = nc.alloc_sbuf_tensor("m2row", [1, T], BF16)
    cs1 = nc.alloc_sbuf_tensor("cs1", [1, NE * EH], BF16)
    mfm = nc.alloc_sbuf_tensor("mfm", [8, T], BF16)
    csgb = nc.alloc_sbuf_tensor("csgb", [1, 8], BF16)
    wgtf = nc.alloc_sbuf_tensor("wgtf", [128, 64], F32)
    wgtb = nc.alloc_sbuf_tensor("wgtb", [128, 64], BF16)
    b2b = nc.alloc_sbuf_tensor("b2b", [8, D], BF16)
    bias1 = nc.alloc_sbuf_tensor("bias1", [128, 16], F32)
    cary = nc.alloc_sbuf_tensor("cary", [128, 8], F32)
    cols = nc.alloc_sbuf_tensor("cols", [128, 180], F32)
    idn = nc.alloc_sbuf_tensor("idn", [128, 128], F32)
    onesb = nc.alloc_sbuf_tensor("onesb", [128, 1], BF16)
    selbb = nc.alloc_sbuf_tensor("selbb", [8, 1024], BF16)
    onesrf = nc.alloc_sbuf_tensor("onesrf", [1, 128], F32)
    msk = nc.alloc_sbuf_tensor("msk", [128, 96], F32)

    psA = nc.alloc_psum_tensor("psA", [128, 2048], F32)
    psB = nc.alloc_psum_tensor("psB", [128, 2048], F32)

    class PBank:

        def __init__(self, t, off):
            self.t, self.off = t, off

        def __getitem__(self, idx):
            rs, cs = idx
            return self.t[rs, self.off + (cs.start or 0): self.off + cs.stop]

    ps = ([PBank(psA, i * 512) for i in range(4)]
          + [PBank(psB, i * 512) for i in range(4)])

    S = Slot
    sl_x = [[S() for _ in range(NTT)] for _ in range(NC)]
    sl_xmm = [[S() for _ in range(NTT)] for _ in range(NC)]
    sl_ps = [S() for _ in range(8)]
    sl_rows = [S() for _ in range(4)]
    sl_m2 = S()
    sl_cs1 = [S() for _ in range(8)]
    sl_mfm = [S() for _ in range(T // 128)]
    sl_msk = [S(), S()]
    sl_stg = [S(), S()]
    sl_htmp = [S() for _ in range(4)]
    sl_gh = [S() for _ in range(8)]
    sl_z = [S() for _ in range(NC)]
    sl_scn = [S(), S()]
    sl_cols = S()
    sl_small = S()
    sl_wgtb = S()
    sl_csgb = S()
    sl_idnb = S()
    sl_lgsb = S()
    sl_b1 = S()
    sl_cary = [S() for _ in range(8)]
    sl_out = S()
    sl_wffn1 = [S() for _ in range(4)]
    sl_wffn2 = [S() for _ in range(4)]
    sl_wgc = [S() for _ in range(4)]
    sl_wsw = [S() for _ in range(SWG * 3)]
    sl_w2s = [S() for _ in range(8)]

    CD = 1.0 / float(D)
    R0, R1, R2S, R3 = 0, TT, 2 * TT, 3 * TT

    def xcol(c, t0):
        return c * T + t0

    def setup():
        sch.emit("d", lambda e: e.dma_start(idn[:, :], ident[:, :]),
                 writes=[sl_small], inc=16)
        sch.emit("d", lambda e: e.dma_start(cols[:, 0:176], pk_t[:, :]),
                 writes=[sl_cols], inc=16)
        sch.emit("d", lambda e: e.dma_start(wgtf[:, :], wgt_t[:, :]),
                 writes=[sl_wgtb], inc=16)
        sch.emit("v", lambda e: e.memset(onesb[:, :], 1.0), writes=[sl_small])
        sch.emit("v", lambda e: e.memset(onesrf[0:1, :], 1.0), writes=[sl_small])
        sch.emit("v", lambda e: e.memset(cols[:, 176:177], 1.0), writes=[sl_cols])
        sch.emit("v", lambda e: e.memset(msk[:, 24:25], 1.0), writes=[sl_msk[0]])
        sch.emit("v", lambda e: e.memset(msk[:, 72:73], 1.0), writes=[sl_msk[1]])
        sch.emit("v", lambda e: e.tensor_scalar_mul(cols[:, 8:16], cols[:, 8:16], -1.0),
                 reads=[sl_cols], writes=[sl_cols])
        sch.emit("s", lambda e: e.activation(wgtb[:, :], wgtf[:, :], AF.Copy),
                 reads=[sl_wgtb], writes=[sl_wgtb])
        sch.emit("d", lambda e: e.dma_start(
            bass.AP(stg, 1024, [[2048, 8], [1, D]]),
            bass.AP(f_b2, 0, [[D, 8], [1, D]])),
            writes=[sl_stg[1]], inc=16)
        sch.emit("s", lambda e: e.activation(b2b[0:8, :], stg[0:8, 1024:2048], AF.Copy),
                 reads=[sl_stg[1]], writes=[sl_small])
        sch.emit("d", lambda e: e.dma_start(
            bass.AP(stg, 1024, [[2048, 8], [1, 1024]]),
            bass.AP(selb, 0, [[1024, 8], [1, 1024]])),
            writes=[sl_stg[1]], inc=16)
        sch.emit("s", lambda e: e.activation(selbb[0:8, :], stg[0:8, 1024:2048], AF.Copy),
                 reads=[sl_stg[1]], writes=[sl_small])

        def mm_csg(e):
            for ci in range(8):
                mm = e.matmul(ps[7][0:1, 0:8], onesb[:, 0:1],
                              wgtb[:, ci * 8:(ci + 1) * 8],
                              start=(ci == 0), stop=(ci == 7))
            return mm
        sch.emit("t", mm_csg, reads=[sl_small, sl_wgtb], writes=[sl_ps[7]])
        sch.emit("s", lambda e: e.activation(csgb[0:1, 0:8], ps[7][0:1, 0:8], AF.Copy),
                 reads=[sl_ps[7]], writes=[sl_csgb])

    def load_x(seq):
        for c in range(NC):
            sch.emit("d" if c % 2 == 0 else "g", lambda e, c=c: e.dma_start(
                bass.AP(xr, xcol(c, 0), [[NC * T, 128], [1, T]]),
                bass.AP(xin, seq * D * T + c * 128 * T, [[T, 128], [1, T]])),
                writes=[sl_x[c][nt] for nt in range(NTT)], inc=16)

    def store_x(seq):
        for c in range(NC):
            sch.emit("d" if c % 2 == 0 else "g", lambda e, seq=seq, c=c: e.dma_start(
                bass.AP(yout, seq * D * T + c * 128 * T, [[T, 128], [1, T]]),
                bass.AP(xr, xcol(c, 0), [[NC * T, 128], [1, T]])),
                reads=[sl_x[c][nt] for nt in range(NTT)], writes=[sl_out],
                inc=16, sem="o")

    def ln_ffn_xmm(sl):
        for lt in range(2):
            nt = sl * 2 + lt
            t0 = nt * TT
            pb_s1, pb_s2, pb_r = 4, 5, 6

            def mm_sum(e, t0=t0):
                for c in range(NC):
                    mm = e.matmul(ps[pb_s1][0:1, 0:TT], cols[:, 176:177],
                                  xr[:, xcol(c, t0): xcol(c, t0) + TT],
                                  start=(c == 0), stop=(c == NC - 1))
                return mm
            sch.emit("t", mm_sum, reads=[sl_cols] + [sl_x[c][nt] for c in range(NC)],
                     writes=[sl_ps[pb_s1]])
            for c in range(NC):
                hb = c % 2
                sch.emit("s", lambda e, c=c, t0=t0, hb=hb: e.activation(
                    htmp[:, hb * TT:(hb + 1) * TT],
                    xr[:, xcol(c, t0): xcol(c, t0) + TT], AF.Square),
                    reads=[sl_x[c][nt]], writes=[sl_htmp[hb]])
                sch.emit("t", lambda e, c=c, hb=hb: e.matmul(
                    ps[pb_s2][0:1, 0:TT], onesb[:, 0:1],
                    htmp[:, hb * TT:(hb + 1) * TT],
                    start=(c == 0), stop=(c == NC - 1)),
                    reads=[sl_htmp[hb], sl_small], writes=[sl_ps[pb_s2]])
            sch.emit("v", lambda e: e.tensor_scalar_mul(
                rows[0:1, R0:R0 + TT], ps[pb_s1][0:1, 0:TT], CD),
                reads=[sl_ps[pb_s1]], writes=[sl_rows[0]])
            sch.emit("v", lambda e: e.tensor_mul(
                rows[0:1, R3:R3 + TT], rows[0:1, R0:R0 + TT], rows[0:1, R0:R0 + TT]),
                reads=[sl_rows[0]], writes=[sl_rows[3]])
            sch.emit("v", lambda e: e.scalar_tensor_tensor(
                rows[0:1, R1:R1 + TT], ps[pb_s2][0:1, 0:TT], CD,
                rows[0:1, R3:R3 + TT], ALU.mult, ALU.subtract),
                reads=[sl_ps[pb_s2], sl_rows[3]], writes=[sl_rows[1]])
            sch.emit("v", lambda e: e.tensor_scalar(
                rows[0:1, R3:R3 + TT], rows[0:1, R1:R1 + TT],
                1.0 + LN_EPS, LN_EPS * LN_EPS, ALU.mult, ALU.add),
                reads=[sl_rows[1]], writes=[sl_rows[3]])
            sch.emit("v", lambda e: e.reciprocal(
                rows[0:1, R3:R3 + TT], rows[0:1, R3:R3 + TT]),
                reads=[sl_rows[3]], writes=[sl_rows[3]])
            sch.emit("s", lambda e: e.activation(
                rows[0:1, R2S:R2S + TT], rows[0:1, R3:R3 + TT], AF.Sqrt),
                reads=[sl_rows[3]], writes=[sl_rows[2]])
            sch.emit("v", lambda e, t0=t0: e.scalar_tensor_tensor(
                m2row[0:1, t0:t0 + TT], rows[0:1, R0:R0 + TT], -1.0,
                rows[0:1, R2S:R2S + TT], ALU.mult, ALU.mult),
                reads=[sl_rows[0], sl_rows[2]], writes=[sl_m2])
            sch.emit("t", lambda e: e.matmul(
                ps[pb_r][:, 0:TT], onesrf[0:1, :], rows[0:1, R2S:R2S + TT],
                start=True, stop=True),
                reads=[sl_rows[2], sl_small], writes=[sl_ps[pb_r]])
            for c in range(NC):
                sch.emit("v", lambda e, c=c, t0=t0: e.tensor_mul(
                    xmm[:, xcol(c, t0): xcol(c, t0) + TT],
                    xr[:, xcol(c, t0): xcol(c, t0) + TT], ps[pb_r][:, 0:TT]),
                    reads=[sl_x[c][nt], sl_ps[pb_r]], writes=[sl_xmm[c][nt]])
            router(nt)

    def router(nt):
        t0 = nt * TT
        pb, pb2 = 7, 6

        def mm_rt(e, t0=t0):
            for c in range(NC):
                e.matmul(ps[pb][0:8, 0:TT], wgtb[:, c * 8:(c + 1) * 8],
                         xmm[:, xcol(c, t0): xcol(c, t0) + TT],
                         start=(c == 0), stop=False)
            return e.matmul(ps[pb][0:8, 0:TT], csgb[0:1, 0:8],
                            m2row[0:1, t0:t0 + TT], start=False, stop=True)
        sch.emit("t", mm_rt,
                 reads=[sl_xmm[c][nt] for c in range(NC)] + [sl_wgtb, sl_csgb, sl_m2],
                 writes=[sl_ps[pb]])
        sch.emit("s", lambda e: e.activation(lgsb[0:8, 0:TT], ps[pb][0:8, 0:TT], AF.Copy),
                 reads=[sl_ps[pb]], writes=[sl_lgsb])
        for sub in range(4):
            i = nt * 4 + sub
            tg = t0 + sub * 128
            pb2 = 3 if sub % 2 == 0 else 2
            mo = 48 * (sub % 2)
            smk = sl_msk[sub % 2]
            sch.emit("t", lambda e, sub=sub, pb2=pb2: e.transpose(
                ps[pb2][0:128, 0:8], lgsb[0:8, sub * 128:(sub + 1) * 128],
                idn[0:8, 0:8]),
                reads=[sl_lgsb, sl_small], writes=[sl_ps[pb2]])
            sch.emit("s", lambda e, pb2=pb2, mo=mo: e.activation(
                msk[:, mo:mo + 8], ps[pb2][:, 0:8], AF.Copy),
                reads=[sl_ps[pb2]], writes=[smk])
            sch.emit("v", lambda e, mo=mo: e.reduce_max(
                msk[:, mo + 46:mo + 47], msk[:, mo:mo + 8], axis=AX.X),
                reads=[smk], writes=[smk])
            sch.emit("v", lambda e, mo=mo: e.tensor_single_scalar(
                msk[:, mo + 8:mo + 16], msk[:, mo:mo + 8],
                msk[:, mo + 46:mo + 47], ALU.is_ge),
                reads=[smk], writes=[smk])
            sch.emit("v", lambda e, mo=mo: e.tensor_scalar(
                msk[:, mo + 16:mo + 24], msk[:, mo + 8:mo + 16],
                -1.0, 1.0, ALU.mult, ALU.add),
                reads=[smk], writes=[smk])
            sch.emit("v", lambda e, mo=mo: e.tensor_tensor_scan(
                msk[:, mo + 25:mo + 33], msk[:, mo + 16:mo + 24],
                msk[:, mo + 16:mo + 24], 1.0, ALU.mult, ALU.bypass),
                reads=[smk], writes=[smk])
            sch.emit("v", lambda e, mo=mo: e.tensor_mul(
                msk[:, mo + 33:mo + 41], msk[:, mo + 8:mo + 16],
                msk[:, mo + 24:mo + 32]),
                reads=[smk], writes=[smk])
            sch.emit("t", lambda e, pb2=pb2, mo=mo: e.transpose(
                ps[pb2][0:8, 128:256], msk[:, mo + 33:mo + 41], idn[:, :]),
                reads=[smk, sl_small], writes=[sl_ps[pb2]])
            sch.emit("s", lambda e, tg=tg, pb2=pb2: e.activation(
                mfm[0:8, tg:tg + 128], ps[pb2][0:8, 128:256], AF.Copy),
                reads=[sl_ps[pb2]], writes=[sl_mfm[i]])

    def ffn(first_call):
        hand_off(sl_wffn1 + sl_wffn2, sl_wgc + sl_wsw + sl_w2s)
        hand_off(sl_gh, sl_z)
        hand_off([sl_lgsb], [sl_scn[0]])
        for sl in range(NSL):
            ln_ffn_xmm(sl)
        def stream_e(e_):
            slot = e_ % 4
            w1c = slot * 4096
            w2c = slot * 4096 + 2048
            for cih in range(2):
                sch.emit("d", lambda e, e2=e_, cih=cih: e.dma_start(
                    bass.AP(stg, 0, [[2048, 128], [1, 1024]]),
                    bass.AP(f_w1, (e2 * 2 + cih) * 128 * D,
                            [[D, 128], [1, 1024]])),
                    writes=[sl_stg[0]], inc=16)
                for c4 in range(4):
                    ci = cih * 4 + c4
                    sch.emit("s", lambda e, c4=c4, e2=e_, ci=ci, w1c=w1c: e.activation(
                        wa[:, w1c + ci * 256: w1c + ci * 256 + 256],
                        stg[:, c4 * 256:(c4 + 1) * 256],
                        AF.Copy, scale=cols[:, 32 + e2 * 8 + ci:33 + e2 * 8 + ci]),
                        reads=[sl_stg[0], sl_cols], writes=[sl_wffn1[slot]])
                if first_call:
                    for hh in range(2):
                        pbb = 5 + hh

                        def mm_b1(e, e2=e_, hh=hh, cih=cih, pbb=pbb):
                            for c4 in range(4):
                                ci = cih * 4 + c4
                                mm = e.matmul(
                                    ps[pbb][:, 0:1],
                                    stg[:, c4 * 256 + hh * 128: c4 * 256 + hh * 128 + 128],
                                    cols[:, 96 + e2 * 8 + ci:97 + e2 * 8 + ci],
                                    start=(ci == 0), stop=(ci == 7))
                            return mm
                        sch.emit("t", mm_b1, reads=[sl_stg[0], sl_cols],
                                 writes=[sl_ps[pbb]])
                        if cih == 1:
                            sch.emit("v", lambda e, e2=e_, hh=hh, pbb=pbb: e.tensor_add(
                                bias1[:, e2 * 2 + hh:e2 * 2 + hh + 1],
                                ps[pbb][:, 0:1],
                                cols[:, 160 + e2 * 2 + hh:161 + e2 * 2 + hh]),
                                reads=[sl_ps[pbb], sl_cols], writes=[sl_b1])
        def stream_w2(e_):
            slot = e_ % 4
            w2c = slot * 4096 + 2048
            for hh in range(2):
                sch.emit("g", lambda e, e2=e_, hh=hh: e.dma_start(
                    bass.AP(stg, 1024, [[2048, 128], [1, D]]),
                    bass.AP(f_w2, (e2 * EH + hh * 128) * D, [[D, 128], [1, D]])),
                    writes=[sl_stg[1]], inc=16)
                sch.emit("s", lambda e, wcol=w2c + hh * 1024: e.activation(
                    wa[:, wcol:wcol + 1024], stg[:, 1024:2048], AF.Copy),
                    reads=[sl_stg[1]], writes=[sl_wffn2[slot]])

        def stage_e(e_):
            slot = e_ % 4
            w1c = slot * 4096
            row0 = slot * 2
            for hh in range(2):
                pbb = 4

                def mm_cs(e, w1c=w1c, hh=hh):
                    for ci in range(8):
                        wcol = w1c + ci * 256 + hh * 128
                        mm = e.matmul(ps[pbb][0:1, 0:128], onesb[:, :],
                                      wa[:, wcol:wcol + 128],
                                      start=(ci == 0), stop=(ci == 7))
                    return mm
                sch.emit("t", mm_cs, reads=[sl_wffn1[slot], sl_small],
                         writes=[sl_ps[pbb]])
                sch.emit("v", lambda e, e2=e_, hh=hh: e.tensor_copy(
                    cs1[0:1, (e2 * 2 + hh) * 128:(e2 * 2 + hh) * 128 + 128],
                    ps[pbb][0:1, 0:128]),
                    reads=[sl_ps[pbb]], writes=[sl_cs1[slot * 2 + hh]])
            for nt in range(NTT):
                sch.emit("t", lambda e, e2=e_, nt=nt: e.matmul(
                    ps[4 + nt][:, 0:TT], selbb[0:8, e2 * 128:(e2 + 1) * 128],
                    mfm[0:8, nt * TT:nt * TT + TT], start=True, stop=True),
                    reads=[sl_mfm[nt * 4 + j] for j in range(4)] + [sl_small],
                    writes=[sl_ps[4 + nt]])
            for hh in range(2):
                gcol = (row0 + hh) * T
                for part in range(2):
                    nts = (2 * part, 2 * part + 1)

                    def mm_s1(e, e2=e_, hh=hh, w1c=w1c, nts=nts):
                        for ci in range(8):
                            wcol = w1c + ci * 256 + hh * 128
                            for nt in nts:
                                e.matmul(ps[nt][:, 0:TT], wa[:, wcol:wcol + 128],
                                         xmm[:, xcol(ci, nt * TT): xcol(ci, nt * TT) + TT],
                                         start=(ci == 0), stop=False)
                        for nt in nts:
                            mm = e.matmul(
                                ps[nt][:, 0:TT],
                                cs1[0:1, (e2 * 2 + hh) * 128:(e2 * 2 + hh) * 128 + 128],
                                m2row[0:1, nt * TT:nt * TT + TT],
                                start=False, stop=True)
                        return mm
                    rd = [sl_wffn1[slot], sl_cs1[slot * 2 + hh], sl_m2]
                    for nt in nts:
                        rd += [sl_xmm[c][nt] for c in range(NC)]
                    sch.emit("t", mm_s1, reads=rd,
                             writes=[sl_ps[nt] for nt in nts])
                    sch.emit("s", lambda e, e2=e_, hh=hh, part=part: e.activation(
                        htmp[:, part * 1024:(part + 1) * 1024],
                        psA[:, part * 1024:(part + 1) * 1024], AF.Gelu,
                        bias=bias1[:, e2 * 2 + hh:e2 * 2 + hh + 1]),
                        reads=[sl_ps[nts[0]], sl_ps[nts[1]], sl_b1],
                        writes=[sl_htmp[nts[0]], sl_htmp[nts[1]]])
                    sch.emit("v", lambda e, gcol=gcol, part=part: e.tensor_mul(
                        ghb[:, gcol + part * 1024: gcol + (part + 1) * 1024],
                        htmp[:, part * 1024:(part + 1) * 1024],
                        psB[:, part * 1024:(part + 1) * 1024]),
                        reads=[sl_htmp[nts[0]], sl_htmp[nts[1]],
                               sl_ps[4 + nts[0]], sl_ps[4 + nts[1]]],
                        writes=[sl_gh[row0 + hh]])

        def y_half(half):
            for dch in range(NC):
                yb = [0, 1, 2, 3] if dch % 2 == 0 else [4, 5, 6, 7]

                def mm_y(e, dch=dch, yb=yb, half=half):
                    for slot in range(4):
                        for hh in range(2):
                            wcol = slot * 4096 + 2048 + hh * 1024 + dch * 128
                            grow = (slot * 2 + hh) * T
                            last1 = (slot == 3 and hh == 1 and half == 0)
                            for nt in range(NTT):
                                mm = e.matmul(
                                    ps[yb[nt]][:, 0:TT], wa[:, wcol:wcol + 128],
                                    ghb[:, grow + nt * TT: grow + nt * TT + TT],
                                    start=(slot == 0 and hh == 0), stop=last1)
                    if half == 1:
                        for nt in range(NTT):
                            mm = e.matmul(ps[yb[nt]][:, 0:TT],
                                          b2b[0:8, dch * 128:dch * 128 + 128],
                                          mfm[0:8, nt * TT:nt * TT + TT],
                                          start=False, stop=True)
                    return mm
                rd = [sl_wffn2[s2] for s2 in range(4)] + [sl_gh[g] for g in range(8)]
                if half == 1:
                    rd = rd + [sl_small] + sl_mfm
                sch.emit("t", mm_y, reads=rd, writes=[sl_ps[b] for b in yb])
                pst = psA if dch % 2 == 0 else psB
                for pr in range(2):
                    sch.emit("v", lambda e, dch=dch, pr=pr, pst=pst: e.scalar_tensor_tensor(
                        xr[:, xcol(dch, pr * 1024): xcol(dch, pr * 1024) + 1024],
                        pst[:, pr * 1024:(pr + 1) * 1024], 0.5,
                        xr[:, xcol(dch, pr * 1024): xcol(dch, pr * 1024) + 1024],
                        ALU.mult, ALU.add),
                        reads=[sl_ps[yb[2 * pr]], sl_ps[yb[2 * pr + 1]],
                               sl_x[dch][2 * pr], sl_x[dch][2 * pr + 1]],
                        writes=[sl_x[dch][2 * pr], sl_x[dch][2 * pr + 1]])

        stream_e(0)
        stream_w2(0)
        stream_e(1)
        stream_w2(1)
        for e_ in range(NE):
            if e_ + 2 < NE:
                stream_e(e_ + 2)
            if e_ + 2 < 4:
                stream_w2(e_ + 2)
            stage_e(e_)
            if e_ == 3:
                y_half(0)
                stream_w2(4)
                stream_w2(5)
            elif e_ in (4, 5):
                stream_w2(e_ + 2)
        y_half(1)

    def slice_stats(src, src_slots, col_of, nt_local, with_sum, pb_s1, pb_s2):
        if with_sum:
            def mm_sum(e, nt_local=nt_local):
                for c in range(NC):
                    off = col_of(c, nt_local * TT)
                    mm = e.matmul(ps[pb_s1][0:1, 0:TT], cols[:, 176:177],
                                  src[:, off:off + TT],
                                  start=(c == 0), stop=(c == NC - 1))
                return mm
            sch.emit("t", mm_sum, reads=[sl_cols] + src_slots,
                     writes=[sl_ps[pb_s1]])
        for c in range(NC):
            hb = c % 2
            off = col_of(c, nt_local * TT)
            sch.emit("s", lambda e, off=off, hb=hb: e.activation(
                htmp[:, hb * TT:(hb + 1) * TT], src[:, off:off + TT], AF.Square),
                reads=src_slots, writes=[sl_htmp[hb]])
            sch.emit("t", lambda e, c=c, hb=hb: e.matmul(
                ps[pb_s2][0:1, 0:TT], onesb[:, 0:1], htmp[:, hb * TT:(hb + 1) * TT],
                start=(c == 0), stop=(c == NC - 1)),
                reads=[sl_htmp[hb], sl_small], writes=[sl_ps[pb_s2]])

    def mixer(seq_unused):
        hand_off(sl_z, sl_gh)
        hand_off([sl_scn[0]], [sl_lgsb])
        for H in range(T // HL):
            tb = H * HL
            ntg = [tb // TT, tb // TT + 1]
            for lt in range(HL // TT):
                nt = ntg[lt]
                pb_s1, pb_s2, pb_r, pb_m = 4, 5, 6, 7
                slice_stats(xr, [sl_x[c][nt] for c in range(NC)], xcol,
                            nt, True, pb_s1, pb_s2)
                sch.emit("v", lambda e: e.tensor_scalar_mul(
                    rows[0:1, R0:R0 + TT], ps[pb_s1][0:1, 0:TT], CD),
                    reads=[sl_ps[pb_s1]], writes=[sl_rows[0]])
                sch.emit("v", lambda e: e.tensor_mul(
                    rows[0:1, R3:R3 + TT], rows[0:1, R0:R0 + TT], rows[0:1, R0:R0 + TT]),
                    reads=[sl_rows[0]], writes=[sl_rows[3]])
                sch.emit("v", lambda e: e.scalar_tensor_tensor(
                    rows[0:1, R1:R1 + TT], ps[pb_s2][0:1, 0:TT], CD,
                    rows[0:1, R3:R3 + TT], ALU.mult, ALU.subtract),
                    reads=[sl_ps[pb_s2], sl_rows[3]], writes=[sl_rows[1]])
                sch.emit("v", lambda e: e.tensor_scalar_add(
                    rows[0:1, R2S:R2S + TT], rows[0:1, R1:R1 + TT], LN_EPS),
                    reads=[sl_rows[1]], writes=[sl_rows[2]])
                sch.emit("v", lambda e: e.reciprocal(
                    rows[0:1, R2S:R2S + TT], rows[0:1, R2S:R2S + TT]),
                    reads=[sl_rows[2]], writes=[sl_rows[2]])
                sch.emit("s", lambda e: e.activation(
                    rows[0:1, R2S:R2S + TT], rows[0:1, R2S:R2S + TT], AF.Sqrt),
                    reads=[sl_rows[2]], writes=[sl_rows[2]])
                sch.emit("v", lambda e: e.scalar_tensor_tensor(
                    rows[0:1, R3:R3 + TT], rows[0:1, R0:R0 + TT], -1.0,
                    rows[0:1, R2S:R2S + TT], ALU.mult, ALU.mult),
                    reads=[sl_rows[0], sl_rows[2]], writes=[sl_rows[3]])
                sch.emit("t", lambda e: e.matmul(
                    ps[pb_r][:, 0:TT], onesrf[0:1, :], rows[0:1, R2S:R2S + TT],
                    start=True, stop=True),
                    reads=[sl_rows[2], sl_small], writes=[sl_ps[pb_r]])
                sch.emit("t", lambda e: e.matmul(
                    ps[pb_m][:, 0:TT], onesrf[0:1, :], rows[0:1, R3:R3 + TT],
                    start=True, stop=True),
                    reads=[sl_rows[3], sl_small], writes=[sl_ps[pb_m]])
                sch.emit("v", lambda e: e.tensor_mul(
                    rows[0:1, R0:R0 + TT], rows[0:1, R2S:R2S + TT],
                    rows[0:1, R2S:R2S + TT]),
                    reads=[sl_rows[2]], writes=[sl_rows[0]])
                sch.emit("v", lambda e: e.tensor_mul(
                    rows[0:1, R0:R0 + TT], rows[0:1, R0:R0 + TT],
                    rows[0:1, R1:R1 + TT]),
                    reads=[sl_rows[0], sl_rows[1]], writes=[sl_rows[0]])
                sch.emit("v", lambda e: e.tensor_scalar_add(
                    rows[0:1, R0:R0 + TT], rows[0:1, R0:R0 + TT], RMS_EPS),
                    reads=[sl_rows[0]], writes=[sl_rows[0]])
                sch.emit("v", lambda e: e.reciprocal(
                    rows[0:1, R0:R0 + TT], rows[0:1, R0:R0 + TT]),
                    reads=[sl_rows[0]], writes=[sl_rows[0]])
                sch.emit("s", lambda e: e.activation(
                    rows[0:1, R0:R0 + TT], rows[0:1, R0:R0 + TT], AF.Sqrt),
                    reads=[sl_rows[0]], writes=[sl_rows[0]])
                sch.emit("t", lambda e: e.matmul(
                    ps[4][:, 0:TT], onesrf[0:1, :], rows[0:1, R0:R0 + TT],
                    start=True, stop=True),
                    reads=[sl_rows[0], sl_small], writes=[sl_ps[4]])
                for c in range(NC):
                    zo = c * HL + lt * TT
                    sch.emit("v", lambda e, c=c, nt=nt, zo=zo: e.tensor_mul(
                        zb[:, zo:zo + TT],
                        xr[:, xcol(c, nt * TT): xcol(c, nt * TT) + TT],
                        ps[pb_r][:, 0:TT]),
                        reads=[sl_x[c][nt], sl_ps[pb_r]], writes=[sl_z[c]])
                    sch.emit("v", lambda e, zo=zo: e.tensor_add(
                        zb[:, zo:zo + TT], zb[:, zo:zo + TT], ps[pb_m][:, 0:TT]),
                        reads=[sl_z[c], sl_ps[pb_m]], writes=[sl_z[c]])
                for c in range(NC):
                    zo = c * HL + lt * TT
                    xo = xcol(c, tb + lt * TT)
                    sch.emit("v", lambda e, c=c, zo=zo, xo=xo: e.scalar_tensor_tensor(
                        xmm[:, xo:xo + TT],
                        zb[:, zo:zo + TT], cols[:, 16 + c:17 + c], ps[4][:, 0:TT],
                        ALU.mult, ALU.mult),
                        reads=[sl_z[c], sl_ps[4], sl_cols], writes=[sl_xmm[c][nt]])
            hand_off(sl_wgc, sl_wffn1 + sl_wffn2 + sl_wsw + sl_w2s)
            for f in range(NC):
                buf = f % 2
                for m, w in enumerate((a_wg, a_wc)):
                    sch.emit("d" if m == 0 else "g",
                             lambda e, w=w, f=f, m=m: e.dma_start(
                        bass.AP(stg, m * 1024, [[2048, 128], [1, 1024]]),
                        bass.AP(w, f * 128 * D, [[D, 128], [1, 1024]])),
                        writes=[sl_stg[m]], inc=16)
                    wcol = buf * 2048 + m * 1024
                    sch.emit("s", lambda e, m=m, wcol=wcol: e.activation(
                        wa[:, wcol:wcol + 1024], stg[:, m * 1024:(m + 1) * 1024],
                        AF.Copy),
                        reads=[sl_stg[m]], writes=[sl_wgc[buf * 2 + m]])
                for m in range(2):
                    wcol = buf * 2048 + m * 1024
                    wslot = sl_wgc[buf * 2 + m]
                    pre = [0 + m * 2, 1 + m * 2]

                    def mm_gc(e, wcol=wcol, pre=pre, tb=tb):
                        for ci in range(8):
                            for k in range(2):
                                mm = e.matmul(
                                    ps[pre[k]][:, 0:TT],
                                    wa[:, wcol + ci * 128:wcol + ci * 128 + 128],
                                    xmm[:, xcol(ci, tb + k * TT): xcol(ci, tb + k * TT) + TT],
                                    start=(ci == 0), stop=(ci == 7))
                        return mm
                    rd = [wslot]
                    for k in range(2):
                        rd += [sl_xmm[c][ntg[k]] for c in range(NC)]
                    sch.emit("t", mm_gc, reads=rd,
                             writes=[sl_ps[pre[0]], sl_ps[pre[1]]])
                    for k in range(2):
                        if m == 0:
                            sch.emit("s", lambda e, f=f, k=k, pb=pre[k]: e.activation(
                                sc2[:, k * TT:k * TT + TT], ps[pb][:, 0:TT],
                                AF.Sigmoid, bias=cols[:, f:f + 1]),
                                reads=[sl_ps[pre[k]], sl_cols], writes=[sl_scn[0]])
                        else:
                            sch.emit("s", lambda e, f=f, k=k, pb=pre[k]: e.activation(
                                sc2[:, HL + k * TT: HL + k * TT + TT], ps[pb][:, 0:TT],
                                AF.Tanh, bias=cols[:, 8 + f:9 + f], scale=-1.0),
                                reads=[sl_ps[pre[k]], sl_cols],
                                writes=[sl_scn[1]])
                sch.emit("v", lambda e: e.scalar_tensor_tensor(
                    sc2[:, HL:2 * HL], sc2[:, 0:HL], 1.0,
                    sc2[:, HL:2 * HL], ALU.subtract, ALU.mult),
                    reads=[sl_scn[0], sl_scn[1]],
                    writes=[sl_scn[1]])
                if H == 0:
                    sch.emit("v", lambda e: e.tensor_tensor_scan(
                        sc2[:, HL:2 * HL], sc2[:, 0:HL],
                        sc2[:, HL:2 * HL], 0.0, ALU.mult, ALU.add),
                        reads=[sl_scn[0], sl_scn[1]],
                        writes=[sl_scn[1]])
                else:
                    sch.emit("v", lambda e, f=f: e.tensor_tensor_scan(
                        sc2[:, HL:2 * HL], sc2[:, 0:HL],
                        sc2[:, HL:2 * HL], cary[:, f:f + 1], ALU.mult, ALU.add),
                        reads=[sl_scn[0], sl_scn[1], sl_cary[f]],
                        writes=[sl_scn[1]])
                if H == 0:
                    sch.emit("v", lambda e, f=f: e.tensor_copy(
                        cary[:, f:f + 1], sc2[:, 2 * HL - 1:2 * HL]),
                        reads=[sl_scn[1]], writes=[sl_cary[f]])
                sch.emit("v", lambda e, f=f: e.tensor_add(
                    zb[:, f * HL:(f + 1) * HL], zb[:, f * HL:(f + 1) * HL],
                    sc2[:, HL:2 * HL]),
                    reads=[sl_scn[1], sl_z[f]], writes=[sl_z[f]])
            for lt in range(HL // TT):
                nt = ntg[lt]
                pb_s2, pb_r = 5, 6
                slice_stats(zb, [sl_z[c] for c in range(NC)],
                            lambda c, t: c * HL + lt * TT, 0, False, 4, pb_s2)
                sch.emit("v", lambda e: e.tensor_scalar_mul(
                    rows[0:1, R1:R1 + TT], ps[pb_s2][0:1, 0:TT], CD),
                    reads=[sl_ps[pb_s2]], writes=[sl_rows[1]])
                sch.emit("v", lambda e: e.tensor_scalar_add(
                    rows[0:1, R2S:R2S + TT], rows[0:1, R1:R1 + TT], RMS_EPS),
                    reads=[sl_rows[1]], writes=[sl_rows[2]])
                sch.emit("v", lambda e: e.reciprocal(
                    rows[0:1, R2S:R2S + TT], rows[0:1, R2S:R2S + TT]),
                    reads=[sl_rows[2]], writes=[sl_rows[2]])
                sch.emit("s", lambda e: e.activation(
                    rows[0:1, R2S:R2S + TT], rows[0:1, R2S:R2S + TT], AF.Sqrt),
                    reads=[sl_rows[2]], writes=[sl_rows[2]])
                sch.emit("t", lambda e: e.matmul(
                    ps[pb_r][:, 0:TT], onesrf[0:1, :], rows[0:1, R2S:R2S + TT],
                    start=True, stop=True),
                    reads=[sl_rows[2], sl_small], writes=[sl_ps[pb_r]])
                for c in range(NC):
                    zo = c * HL + lt * TT
                    xo = xcol(c, tb + lt * TT)
                    sch.emit("v", lambda e, c=c, zo=zo, xo=xo: e.scalar_tensor_tensor(
                        xmm[:, xo:xo + TT],
                        zb[:, zo:zo + TT], cols[:, 24 + c:25 + c], ps[pb_r][:, 0:TT],
                        ALU.mult, ALU.mult),
                        reads=[sl_z[c], sl_ps[pb_r], sl_cols], writes=[sl_xmm[c][nt]])
            for c in range(NC):
                for lt in range(HL // TT):
                    nt = ntg[lt]
                    zo = c * HL + lt * TT
                    sch.emit("v" if c % 2 == 0 else "g",
                             lambda e, c=c, nt=nt, zo=zo: e.tensor_add(
                        xr[:, xcol(c, nt * TT): xcol(c, nt * TT) + TT],
                        xr[:, xcol(c, nt * TT): xcol(c, nt * TT) + TT],
                        zb[:, zo:zo + TT]),
                        reads=[sl_z[c], sl_x[c][nt]], writes=[sl_x[c][nt]])
        hand_off(sl_wsw + sl_w2s, sl_wgc + sl_wffn1 + sl_wffn2)
        hand_off(sl_gh, sl_z)
        nj = DFF // 128
        for g in range(nj // 4):
            grp = list(range(g * 4, g * 4 + 4))
            for j in grp:
                slot = j % 4
                wbase = slot * 2048
                w2base = 8192 + (g % 2) * 4096 + slot * 1024
                for m, w in enumerate((a_w1, a_w3)):
                    sch.emit("d" if m == 0 else "g",
                             lambda e, w=w, j=j, m=m: e.dma_start(
                        bass.AP(stg, m * 1024, [[2048, 128], [1, 1024]]),
                        bass.AP(w, j * 128 * D, [[D, 128], [1, 1024]])),
                        writes=[sl_stg[m]], inc=16)
                    sch.emit("s", lambda e, wbase=wbase, m=m: e.activation(
                        wa[:, wbase + m * 1024: wbase + m * 1024 + 1024],
                        stg[:, m * 1024:(m + 1) * 1024], AF.Copy),
                        reads=[sl_stg[m]], writes=[sl_wsw[slot * 3 + m]])
                sch.emit("d" if j % 2 == 0 else "g",
                         lambda e, j=j: e.dma_start(
                    bass.AP(stg, 0, [[2048, 128], [1, D]]),
                    bass.AP(a_w2, j * 128 * D, [[D, 128], [1, D]])),
                    writes=[sl_stg[0]], inc=16)
                sch.emit("s", lambda e, w2base=w2base: e.activation(
                    wa[:, w2base: w2base + 1024],
                    stg[:, 0:1024], AF.Copy),
                    reads=[sl_stg[0]], writes=[sl_w2s[(g % 2) * 4 + slot]])
            for j in grp:
                slot = j % 4
                wbase = slot * 2048
                grow = (g % 2) * 4 + slot
                pA = [0, 1, 2, 3]
                pB = [4, 5, 6, 7]

                def mm_pre(e, wofs, pbs):
                    for ci in range(8):
                        for nt in range(NTT):
                            mm = e.matmul(
                                ps[pbs[nt]][:, 0:TT],
                                wa[:, wofs + ci * 128: wofs + ci * 128 + 128],
                                xmm[:, xcol(ci, nt * TT): xcol(ci, nt * TT) + TT],
                                start=(ci == 0), stop=(ci == 7))
                    return mm
                rdx = []
                for nt in range(NTT):
                    rdx += [sl_xmm[c][nt] for c in range(NC)]
                sch.emit("t", lambda e, f_=mm_pre, wb=wbase, pA=pA: f_(e, wb, pA),
                         reads=[sl_wsw[slot * 3]] + rdx,
                         writes=[sl_ps[b] for b in pA])
                sch.emit("s", lambda e: e.activation(
                    htmp[:, 0:T], psA[:, 0:T], AF.Silu),
                    reads=[sl_ps[b] for b in pA], writes=sl_htmp)
                sch.emit("t", lambda e, f_=mm_pre, wb=wbase, pB=pB: f_(e, wb + 1024, pB),
                         reads=[sl_wsw[slot * 3 + 1]] + rdx,
                         writes=[sl_ps[b] for b in pB])
                sch.emit("v", lambda e, grow=grow: e.tensor_mul(
                    ghb[:, grow * T: grow * T + T], htmp[:, 0:T], psB[:, 0:T]),
                    reads=sl_htmp + [sl_ps[b] for b in pB], writes=[sl_gh[grow]])
            for dch in range(NC):
                yb = [0, 1, 2, 3] if dch % 2 == 0 else [4, 5, 6, 7]

                def mm_y(e, dch=dch, grp=grp, yb=yb, g=g):
                    for jj, j in enumerate(grp):
                        slot = j % 4
                        wofs = 8192 + (g % 2) * 4096 + slot * 1024 + dch * 128
                        grow = (g % 2) * 4 + slot
                        for nt in range(NTT):
                            mm = e.matmul(
                                ps[yb[nt]][:, 0:TT], wa[:, wofs:wofs + 128],
                                ghb[:, grow * T + nt * TT: grow * T + nt * TT + TT],
                                start=(jj == 0), stop=(jj == len(grp) - 1))
                    return mm
                sch.emit("t", mm_y,
                         reads=[sl_w2s[(g % 2) * 4 + j % 4] for j in grp] +
                               [sl_gh[(g % 2) * 4 + j % 4] for j in grp],
                         writes=[sl_ps[b] for b in yb])
                pst = psA if dch % 2 == 0 else psB
                for pr in range(2):
                    sch.emit("v", lambda e, dch=dch, pr=pr, pst=pst: e.tensor_add(
                        xr[:, xcol(dch, pr * 1024): xcol(dch, pr * 1024) + 1024],
                        xr[:, xcol(dch, pr * 1024): xcol(dch, pr * 1024) + 1024],
                        pst[:, pr * 1024:(pr + 1) * 1024]),
                        reads=[sl_ps[yb[2 * pr]], sl_ps[yb[2 * pr + 1]],
                               sl_x[dch][2 * pr], sl_x[dch][2 * pr + 1]],
                        writes=[sl_x[dch][2 * pr], sl_x[dch][2 * pr + 1]])

    setup()
    for seq in range(NSEQ):
        load_x(seq)
        if upto >= 1:
            ffn(first_call=(seq == 0))
        if upto >= 2:
            mixer(seq)
        if upto >= 4:
            ffn(first_call=False)
        store_x(seq)
    sch.final_output_waits("d")

    import contextlib
    with contextlib.ExitStack() as stack:
        sems = {s: stack.enter_context(nc.semaphore(s)) for s in ALL_SEMS}
        with nc.Block() as block:
            def mk(eng_key):
                def body(eng):
                    for waits, build, sem, inc in sch.prog[eng_key]:
                        for s, v in waits:
                            eng.wait_ge(sems[s], v)
                        if build is not None:
                            inst = build(eng)
                            if inc:
                                inst.then_inc(sems[sem], inc)
                return body
            block.sync(mk("d"))
            block.gpsimd(mk("g"))
            block.tensor(mk("t"))
            block.scalar(mk("s"))
            block.vector(mk("v"))
    return nc


def kernel(L0_attn_x, L0_attn_Wg, L0_attn_bg, L0_attn_Wc, L0_attn_bc,
           L0_attn_n1_w, L0_attn_n2_w, L0_attn_W1, L0_attn_W3, L0_attn_W2,
           L0_ffn_Wg, L0_ffn_ln_scale, L0_ffn_ln_bias, L0_ffn_W1, L0_ffn_b1,
           L0_ffn_W2, L0_ffn_b2):
    x = np.ascontiguousarray(np.asarray(L0_attn_x, dtype=np.float32))
    B = x.shape[0]
    ncores = 8
    per = B // ncores

    def f32(v):
        return np.asarray(v, np.float32)

    pk = np.empty((128, 176), np.float32)
    pk[:, 0:8] = f32(L0_attn_bg).reshape(8, 128).T
    pk[:, 8:16] = f32(L0_attn_bc).reshape(8, 128).T
    pk[:, 16:24] = f32(L0_attn_n1_w).reshape(8, 128).T
    pk[:, 24:32] = f32(L0_attn_n2_w).reshape(8, 128).T
    pk[:, 32:96] = f32(L0_ffn_ln_scale).reshape(8, 8, 128).transpose(2, 0, 1).reshape(128, 64)
    pk[:, 96:160] = f32(L0_ffn_ln_bias).reshape(8, 8, 128).transpose(2, 0, 1).reshape(128, 64)
    pk[:, 160:176] = f32(L0_ffn_b1).reshape(8, 2, 128).transpose(2, 0, 1).reshape(128, 16)
    wgt = np.ascontiguousarray(
        f32(L0_ffn_Wg).reshape(8, 8, 128).transpose(2, 1, 0).reshape(128, 64))
    def dchunks(w):
        w = f32(w)
        n = w.shape[1]
        return np.ascontiguousarray(
            w.reshape(8, 128, n // 128, 128).transpose(2, 1, 0, 3)
            .reshape(n // 128, 128, 8 * 128))

    common = {
        "a_wg": dchunks(L0_attn_Wg),
        "a_wc": dchunks(L0_attn_Wc),
        "a_w1": dchunks(L0_attn_W1),
        "a_w3": dchunks(L0_attn_W3),
        "a_w2": np.ascontiguousarray(L0_attn_W2, np.float32),
        "f_w1": np.ascontiguousarray(
            f32(L0_ffn_W1).reshape(8, 2, 4, 128, 256).transpose(0, 1, 3, 2, 4)
            .reshape(8, 2, 128, 1024)),
        "f_w2": np.ascontiguousarray(L0_ffn_W2, np.float32),
        "f_b2": np.ascontiguousarray(L0_ffn_b2, np.float32),
        "pk": pk,
        "wgt": wgt,
        "ident": np.eye(128, dtype=np.float32),
        "selb": np.kron(np.eye(8, dtype=np.float32), np.ones((1, 128), np.float32)),
    }
    nc = build_program()
    in_maps = []
    for c in range(ncores):
        m = dict(common)
        m["xin"] = np.ascontiguousarray(
            x[c * per:(c + 1) * per].transpose(0, 2, 1))
        in_maps.append(m)
    res = run_bass_kernel_spmd(nc, in_maps, core_ids=list(range(ncores)))
    return np.concatenate(
        [r["yout"].transpose(0, 2, 1) for r in res.results], axis=0)
